# version 05
# baseline (speedup 1.0000x reference)
import numpy as np
from contextlib import ExitStack

import concourse.bass as bass
import concourse.mybir as mybir
from concourse.bass_utils import run_bass_kernel_spmd

F32 = mybir.dt.float32
F16 = mybir.dt.float16
AF = mybir.ActivationFunctionType
ALU = mybir.AluOpType

D = 1024
KC = 8
NS = 256
T = 4096
DF = 2736
DH = 4096
TS = 512
NTS = T // TS
EPS_LN = 1e-5
EPS_RMS = 1e-6

FT_W = [128] * 21 + [48]
FT_OFF = [sum(FT_W[:i]) for i in range(len(FT_W))]
BQ = [list(range(0, 6)), list(range(6, 12)), list(range(12, 17)), list(range(17, 22))]
CQ = [list(range(8 * q, 8 * q + 8)) for q in range(4)]
TGB = 512
NTGB = T // TGB
TGC = 512
NTGC = T // TGC


def _membrane_breakpoints():
    def g(y):
        y = np.asarray(y, dtype=np.float64)
        mem = np.zeros_like(y)
        out = np.zeros_like(y)
        for _ in range(5):
            reset = (mem > 1.0).astype(np.float64)
            mem = 0.9 * mem + y - reset
            out += (mem > 1.0)
        return out

    ys = np.linspace(-1.0, 2.0, 600001)
    vals = g(ys)
    jumps = np.nonzero(np.diff(vals) != 0)[0]
    assert len(jumps) == 5, len(jumps)
    out = []
    for j in jumps:
        lo, hi = float(ys[j]), float(ys[j + 1])
        vlo = g(np.array([lo]))[0]
        for _ in range(90):
            mid = 0.5 * (lo + hi)
            if g(np.array([mid]))[0] == vlo:
                lo = mid
            else:
                hi = mid
        b_hi = np.float32(lo)
        b_lo = np.float32(np.float64(lo) - np.float64(b_hi))
        out.append((float(b_hi), float(b_lo)))
    return out


_BPS = _membrane_breakpoints()


class Ctr:
    def __init__(self, stack, nc, name):
        self.sem = stack.enter_context(nc.semaphore(name))
        self.n = 0

    def inc(self, k=1):
        self.n += k
        return self.n


class Arena:
    def __init__(self, t, cols):
        self.t = t
        self.cols = cols
        self.off = 0

    def alloc(self, cols):
        assert self.off + cols <= self.cols, (self.off, cols, self.cols)
        ap = self.t[:, self.off:self.off + cols]
        self.off += cols
        return ap

    def mark(self):
        return self.off

    def reset(self, m):
        self.off = m


class Builder:
    def __init__(self, dbg=False):
        import os
        nc = bass.Bass("TRN2", target_bir_lowering=False,
                       detect_race_conditions=bool(os.environ.get("K_RD")))
        self.nc = nc
        dt = nc.dram_tensor
        self.i_x = dt("x", [T, D], F32, kind="ExternalInput").ap()
        self.i_xp = dt("xprev", [T, D], F32, kind="ExternalInput").ap()
        self.i_bwt = dt("BwT", [D, NS], F32, kind="ExternalInput").ap()
        self.i_cwt = dt("CwT", [NS, D], F32, kind="ExternalInput").ap()
        self.i_w1t = dt("w1T", [D, DF], F32, kind="ExternalInput").ap()
        self.i_w3t = dt("w3T", [D, DF], F32, kind="ExternalInput").ap()
        self.i_w2t = dt("w2T", [DF, D], F32, kind="ExternalInput").ap()
        self.i_fct = dt("fcT", [D, DH], F32, kind="ExternalInput").ap()
        self.i_pjt = dt("projT", [DH, D], F32, kind="ExternalInput").ap()
        self.i_ll = dt("log_lambda", [NS], F32, kind="ExternalInput").ap()
        self.i_dsk = dt("D_skip", [D], F32, kind="ExternalInput").ap()
        self.i_wsm = dt("ssm_w", [D], F32, kind="ExternalInput").ap()
        self.i_wff = dt("ffn_w", [D], F32, kind="ExternalInput").ap()
        self.o_out = dt("out", [T, D], F32, kind="ExternalOutput").ap()
        kd = "ExternalOutput" if dbg else "Internal"
        self.s_uh = dt("s_uh", [T, D], F16, kind=kd).ap()
        self.s_ul = dt("s_ul", [T, D], F16, kind=kd).ap()
        self.s_uhp = dt("s_uhp", [T, D], F16, kind=kd).ap()
        self.s_ulp = dt("s_ulp", [T, D], F16, kind=kd).ap()
        self.s_z2h = dt("s_z2h", [T, D], F16, kind=kd).ap()
        self.s_z2l = dt("s_z2l", [T, D], F16, kind=kd).ap()
        self.s_z3h = dt("s_z3h", [T, D], F16, kind=kd).ap()
        self.s_z3l = dt("s_z3l", [T, D], F16, kind=kd).ap()
        self.s_x1 = dt("s_x1", [T, D], F32, kind=kd).ap()
        self.s_x2 = dt("s_x2", [T, D], F32, kind=kd).ap()
        self.s_pb = dt("s_pb", [T, D], F32, kind=kd).ap()
        self.s_yp = dt("s_yp", [T, D], F32, kind=kd).ap()

        self.stack = ExitStack()
        st = self.stack
        self.a32 = Arena(st.enter_context(nc.sbuf_tensor("a32", [128, 16128], F32)), 16128)
        self.a16 = Arena(st.enter_context(nc.sbuf_tensor("a16", [128, 64512], F16)), 64512)
        self.ps = [st.enter_context(nc.psum_tensor(f"ps{i}", [128, 512], F32)).ap()
                   for i in range(8)]
        self.bar_sem = st.enter_context(nc.semaphore("bar"))
        self.bar_n = 0
        self.ld = Ctr(st, nc, "ld")
        self.tr = Ctr(st, nc, "tr")
        self.pf = Ctr(st, nc, "pf")
        self.wl = Ctr(st, nc, "wl")
        self.st = Ctr(st, nc, "st")
        self.ted = [Ctr(st, nc, f"ted{i}") for i in range(8)]
        self.psf = [Ctr(st, nc, f"psf{i}") for i in range(8)]
        self.vs = Ctr(st, nc, "vs")
        self.ac = Ctr(st, nc, "ac")
        self.gp = Ctr(st, nc, "gp")
        self.tf = Ctr(st, nc, "tf")
        self.uses = [0] * 8

        cdata = np.tile(np.array([EPS_LN, EPS_RMS, 1.0 / D, 0.0],
                                 dtype=np.float32), (128, 1))
        self.i_cst = nc.inline_tensor(cdata, name="consts").ap()
        self.cst = self.a32.alloc(4)
        self.stM = self.a32.alloc(32)
        self.stR = self.a32.alloc(32)
        self.stR2 = self.a32.alloc(32)
        self.lam = self.a32.alloc(2)
        self.carry = self.a32.alloc(2)
        self.dws = self.a32.alloc(D)
        self.wsm8 = self.a32.alloc(8)
        self.ffw8 = self.a32.alloc(8)
        self.tiny = self.a32.alloc(64)
        self.dump = self.a32.alloc(1024)

    def barrier(self):
        nc = self.nc
        self.bar_n += 5
        tgt = self.bar_n
        with nc.Block() as blk:
            for name in ("sync", "vector", "scalar", "tensor", "gpsimd"):
                def body(eng, t=tgt):
                    eng.drain(fusable=False).then_inc(self.bar_sem, 1)
                    eng.wait_ge(self.bar_sem, t)
                getattr(blk, name)(body)

    def bank_acquire(self, te, b, use):
        if use > 1:
            te.wait_ge(self.psf[b].sem, use - 1)

    def phase_a_init(self):
        nc = self.nc
        a32, a16 = self.a32, self.a16
        self.bw_h = a16.alloc(2048)
        self.bw_l = a16.alloc(2048)
        self.cw_h = a16.alloc(2048)
        self.cw_l = a16.alloc(2048)
        m32 = a32.mark()
        wst = a32.alloc(2048)
        lamraw = a32.alloc(2)
        dskb = a32.alloc(D)
        wsmb = a32.alloc(D)

        ld, vs, ac = self.ld, self.vs, self.ac
        t_ld1 = ld.inc(16 * 7)
        t_vs1 = vs.inc(1)
        t_ld2 = ld.inc(16)
        t_vs2 = vs.inc(1)
        t_ac1 = ac.inc(1)

        with nc.Block() as blk:
            @blk.sync
            def _(s):
                s.dma_start(out=lamraw[:, :],
                            in_=self.i_ll.rearrange("(c p) -> p c", p=128),
                            allow_slow_non_contiguous=True
                            ).then_inc(ld.sem, 16)
                s.dma_start(out=self.wsm8[:, :],
                            in_=self.i_wsm.rearrange("(k p) -> p k", p=128),
                            allow_slow_non_contiguous=True
                            ).then_inc(ld.sem, 16)
                s.dma_start(out=self.ffw8[:, :],
                            in_=self.i_wff.rearrange("(k p) -> p k", p=128),
                            allow_slow_non_contiguous=True
                            ).then_inc(ld.sem, 16)
                s.dma_start(out=dskb[:, :],
                            in_=self.i_dsk.unsqueeze(0).broadcast_to([128, D])
                            ).then_inc(ld.sem, 16)
                s.dma_start(out=wsmb[:, :],
                            in_=self.i_wsm.unsqueeze(0).broadcast_to([128, D])
                            ).then_inc(ld.sem, 16)
                s.dma_start(out=self.cst[:, :], in_=self.i_cst
                            ).then_inc(ld.sem, 16)
                s.dma_start(out=wst[:, :].rearrange("p (k n) -> p k n", k=KC),
                            in_=self.i_bwt.rearrange("(k p) n -> p k n", p=128)
                            ).then_inc(ld.sem, 16)
                s.wait_ge(vs.sem, t_vs1)
                s.dma_start(out=wst[:, :].rearrange("p (c d) -> p c d", c=2),
                            in_=self.i_cwt.rearrange("(c p) d -> p c d", p=128)
                            ).then_inc(ld.sem, 16)

            @blk.scalar
            def _(sc):
                sc.wait_ge(ld.sem, t_ld1)
                sc.activation(self.lam[:, :], lamraw[:, :], AF.Sigmoid
                              )
                sc.drain(fusable=False).then_inc(ac.sem, 1)

            @blk.vector
            def _(v):
                v.wait_ge(ld.sem, t_ld1)
                v.tensor_mul(self.dws[:, :], wsmb[:, :], dskb[:, :])
                for k in range(KC):
                    w = wst[:, k * 256:(k + 1) * 256]
                    hi = self.bw_h[:, k * 256:(k + 1) * 256]
                    lo = self.bw_l[:, k * 256:(k + 1) * 256]
                    v.tensor_scalar(hi, w, self.wsm8[:, k:k + 1], None,
                                    op0=ALU.mult)
                    v.drain(fusable=False)
                    i = v.scalar_tensor_tensor(lo, w, self.wsm8[:, k:k + 1], hi,
                                               op0=ALU.mult, op1=ALU.subtract)
                v.drain(fusable=False).then_inc(vs.sem, 1)
                v.wait_ge(ld.sem, t_ld2)
                v.tensor_copy(self.cw_h[:, :], wst[:, :])
                v.drain(fusable=False)
                v.tensor_sub(self.cw_l[:, :], wst[:, :], self.cw_h[:, :]
                             )
                v.drain(fusable=False).then_inc(vs.sem, 1)
        a32.reset(m32)

    def phase_a1(self, src):
        nc = self.nc
        a32, a16 = self.a32, self.a16
        m32, m16 = a32.mark(), a16.mark()
        xsl = [a32.alloc(4096), a32.alloc(4096)]
        z1b = a32.alloc(4096)
        usl = [(a16.alloc(4096), a16.alloc(4096)) for _ in range(2)]
        sS = self.tiny[:, 0:4]
        sQ = self.tiny[:, 4:8]
        sV = self.tiny[:, 8:12]
        sSD = self.tiny[:, 12:16]
        sQ2 = self.tiny[:, 16:20]
        sSD2 = self.tiny[:, 20:24]
        sVa = self.tiny[:, 40:44]

        x_src = self.i_x if src else self.i_xp
        uh_st = self.s_uh if src else self.s_uhp
        ul_st = self.s_ul if src else self.s_ulp

        ld, vs, ac, st = self.ld, self.vs, self.ac, self.st
        serial = getattr(self, "a1_serial", False)
        sch = []
        for ts in range(NTS):
            it = {"ts": ts, "slot": ts % 2,
                  "ld": ld.inc(16),
                  "ac1": ac.inc(1), "vs1": vs.inc(1),
                  "ac2": ac.inc(1), "vs2": vs.inc(1),
                  "ac3": ac.inc(1), "vs3": vs.inc(1),
                  "st": st.inc(32)}
            sch.append(it)

        with nc.Block() as blk:
            @blk.sync
            def _(s):
                for it in sch:
                    ts, sl = it["ts"], it["slot"]
                    if serial and ts >= 1:
                        s.wait_ge(st.sem, sch[ts - 1]["st"])
                    if ts >= 2:
                        s.wait_ge(vs.sem, sch[ts - 2]["vs2"])
                    s.dma_start(
                        out=xsl[sl][:, :].rearrange("p (a d) -> p a d", a=4),
                        in_=x_src[ts * TS:(ts + 1) * TS, :].rearrange(
                            "(a p) d -> p a d", p=128)).then_inc(ld.sem, 16)

            @blk.scalar
            def _(sc):
                for it in sch:
                    ts, sl = it["ts"], it["slot"]
                    if serial and ts >= 1:
                        sc.wait_ge(st.sem, sch[ts - 1]["st"])
                    sc.wait_ge(ld.sem, it["ld"])
                    for r in range(4):
                        xt = xsl[sl][:, r * 1024:(r + 1) * 1024]
                        sc.activation(self.dump[:, :], xt, AF.Identity,
                                      accum_out=sS[:, r:r + 1])
                    for r in range(4):
                        xt = xsl[sl][:, r * 1024:(r + 1) * 1024]
                        sc.activation(self.dump[:, :], xt, AF.Square,
                                      accum_out=sQ[:, r:r + 1])
                    sc.drain(fusable=False).then_inc(ac.sem, 1)
                    sc.wait_ge(vs.sem, it["vs1"])
                    sc.activation(sSD[:, :], sV[:, :], AF.Sqrt,
                                  bias=self.cst[:, 0:1])
                    sc.drain(fusable=False).then_inc(ac.sem, 1)
                    sc.wait_ge(vs.sem, it["vs2"])
                    for r in range(4):
                        z1 = z1b[:, r * 1024:(r + 1) * 1024]
                        sc.activation(self.dump[:, :], z1, AF.Square,
                                      accum_out=sQ2[:, r:r + 1])
                    sc.drain(fusable=False)
                    sc.activation(sSD2[:, :], sQ2[:, :], AF.Sqrt,
                                  bias=self.cst[:, 1:2],
                                  scale=self.cst[:, 2:3])
                    sc.drain(fusable=False).then_inc(ac.sem, 1)

            @blk.vector
            def _(v):
                for it in sch:
                    ts, sl = it["ts"], it["slot"]
                    c0 = ts * 4
                    if serial and ts >= 1:
                        v.wait_ge(st.sem, sch[ts - 1]["st"])
                    v.wait_ge(ac.sem, it["ac1"])
                    v.tensor_single_scalar(self.stM[:, c0:c0 + 4], sS[:, :],
                                           1.0 / D, op=ALU.mult)
                    v.drain(fusable=False)
                    v.tensor_mul(sVa[:, :], self.stM[:, c0:c0 + 4],
                                 self.stM[:, c0:c0 + 4])
                    v.drain(fusable=False)
                    v.scalar_tensor_tensor(sV[:, :], sQ[:, :], 1.0 / D,
                                           sVa[:, :], op0=ALU.mult,
                                           op1=ALU.subtract)
                    v.drain(fusable=False).then_inc(vs.sem, 1)
                    v.wait_ge(ac.sem, it["ac2"])
                    v.reciprocal(self.stR[:, c0:c0 + 4], sSD[:, :])
                    v.drain(fusable=False)
                    last = None
                    for r in range(4):
                        xt = xsl[sl][:, r * 1024:(r + 1) * 1024]
                        z1 = z1b[:, r * 1024:(r + 1) * 1024]
                        last = v.tensor_scalar(z1, xt,
                                               self.stM[:, c0 + r:c0 + r + 1],
                                               self.stR[:, c0 + r:c0 + r + 1],
                                               op0=ALU.subtract, op1=ALU.mult)
                    v.drain(fusable=False).then_inc(vs.sem, 1)
                    v.wait_ge(ac.sem, it["ac3"])
                    v.reciprocal(self.stR2[:, c0:c0 + 4], sSD2[:, :])
                    v.drain(fusable=False)
                    if ts >= 2:
                        v.wait_ge(st.sem, sch[ts - 2]["st"])
                    uh, ul = usl[sl]
                    last = None
                    for r in range(4):
                        z1 = z1b[:, r * 1024:(r + 1) * 1024]
                        r2 = self.stR2[:, c0 + r:c0 + r + 1]
                        v.tensor_scalar(uh[:, r * 1024:(r + 1) * 1024], z1, r2,
                                        None, op0=ALU.mult)
                        last = v.scalar_tensor_tensor(
                            ul[:, r * 1024:(r + 1) * 1024], z1, r2,
                            uh[:, r * 1024:(r + 1) * 1024],
                            op0=ALU.mult, op1=ALU.subtract)
                    v.drain(fusable=False).then_inc(vs.sem, 1)

            @blk.gpsimd
            def _(g):
                for it in sch:
                    ts, sl = it["ts"], it["slot"]
                    g.wait_ge(vs.sem, it["vs3"])
                    uh, ul = usl[sl]
                    g.dma_start(
                        out=uh_st[ts * TS:(ts + 1) * TS, :].rearrange(
                            "(a p) d -> p a d", p=128),
                        in_=uh[:, :].rearrange("p (a d) -> p a d", a=4)
                        ).then_inc(st.sem, 16)
                    g.dma_start(
                        out=ul_st[ts * TS:(ts + 1) * TS, :].rearrange(
                            "(a p) d -> p a d", p=128),
                        in_=ul[:, :].rearrange("p (a d) -> p a d", a=4)
                        ).then_inc(st.sem, 16)

        a32.reset(m32)
        a16.reset(m16)

    def phase_a2(self, src):
        nc = self.nc
        a32, a16 = self.a32, self.a16
        m32, m16 = a32.mark(), a16.mark()
        ufh = [a16.alloc(4096), a16.alloc(4096)]
        ufl = [a16.alloc(4096), a16.alloc(4096)]
        bu32 = a32.alloc(2048)
        hs32 = a32.alloc(2048)

        uh_st = self.s_uh if src else self.s_uhp
        ul_st = self.s_ul if src else self.s_ulp
        tr, vs, tf = self.tr, self.vs, self.tf
        st0 = self.st.n
        BU = [6, 7]

        sch = []
        for ts in range(NTS):
            it = {"ts": ts, "slot": ts % 2, "tr": tr.inc(16 * 16), "use": []}
            for nc2 in range(2):
                b = BU[nc2]
                self.uses[b] += 1
                it["use"].append(self.uses[b])
            it["vs"] = vs.inc(1)
            it["tf"] = tf.inc(1)
            sch.append(it)

        with nc.Block() as blk:
            @blk.sync
            def _(s):
                if st0:
                    s.wait_ge(self.st.sem, st0)
                for it in sch:
                    ts, sl = it["ts"], it["slot"]
                    if ts >= 2:
                        s.wait_ge(tf.sem, sch[ts - 2]["tf"])
                    for k in range(KC):
                        s.dma_start(out=ufh[sl][:, k * 512:(k + 1) * 512],
                                    in_=uh_st[ts * TS:(ts + 1) * TS,
                                              k * 128:(k + 1) * 128],
                                    transpose=True).then_inc(tr.sem, 16)
                        s.dma_start(out=ufl[sl][:, k * 512:(k + 1) * 512],
                                    in_=ul_st[ts * TS:(ts + 1) * TS,
                                              k * 128:(k + 1) * 128],
                                    transpose=True).then_inc(tr.sem, 16)

            @blk.tensor
            def _(te):
                for it in sch:
                    ts, sl = it["ts"], it["slot"]
                    te.wait_ge(tr.sem, it["tr"])
                    for nc2 in range(2):
                        b = BU[nc2]
                        use = it["use"][nc2]
                        self.bank_acquire(te, b, use)
                        last = None
                        for k in range(KC):
                            bh = self.bw_h[:, k * 256 + nc2 * 128:
                                           k * 256 + nc2 * 128 + 128]
                            bl = self.bw_l[:, k * 256 + nc2 * 128:
                                           k * 256 + nc2 * 128 + 128]
                            xh = ufh[sl][:, k * 512:(k + 1) * 512]
                            xl = ufl[sl][:, k * 512:(k + 1) * 512]
                            te.ldweights(bh)
                            te.matmul(self.ps[b][:, :], bh, xh,
                                      start=(k == 0), stop=False)
                            te.matmul(self.ps[b][:, :], bh, xl,
                                      start=False, stop=False)
                            te.ldweights(bl)
                            last = te.matmul(self.ps[b][:, :], bl, xh,
                                             start=False, stop=(k == KC - 1))
                        last.then_inc(self.ted[b].sem, 1)
                    te.nop().then_inc(tf.sem, 1)

            @blk.vector
            def _(v):
                for it in sch:
                    ts, sl = it["ts"], it["slot"]
                    for nc2 in range(2):
                        b = BU[nc2]
                        use = it["use"][nc2]
                        v.wait_ge(self.ted[b].sem, use)
                        bu = bu32[:, (sl * 2 + nc2) * 512:(sl * 2 + nc2 + 1) * 512]
                        hs = hs32[:, (sl * 2 + nc2) * 512:(sl * 2 + nc2 + 1) * 512]
                        v.tensor_copy(bu, self.ps[b][:, :]
                                      ).then_inc(self.psf[b].sem, 1)
                        if ts == 0:
                            init = self.carry[:, nc2:nc2 + 1] if src else 0.0
                        else:
                            psl = 1 - sl
                            init = hs32[:, (psl * 2 + nc2) * 512 + 511:
                                        (psl * 2 + nc2) * 512 + 512]
                        v.tensor_tensor_scan(
                            hs, self.lam[:, nc2:nc2 + 1].broadcast_to([128, 512]),
                            bu, init, op0=ALU.mult, op1=ALU.add)
                        if src:
                            hh = self.hsp_h[:, nc2 * T + ts * 512:
                                            nc2 * T + (ts + 1) * 512]
                            hl = self.hsp_l[:, nc2 * T + ts * 512:
                                            nc2 * T + (ts + 1) * 512]
                            v.tensor_copy(hh, hs)
                            v.drain(fusable=False)
                            v.scalar_tensor_tensor(hl, hs, 1.0, hh,
                                                   op0=ALU.mult,
                                                   op1=ALU.subtract)
                        elif ts == NTS - 1:
                            v.drain(fusable=False)
                            v.tensor_copy(self.carry[:, nc2:nc2 + 1],
                                          hs[:, 511:512])
                    v.nop()
                    v.drain(fusable=False).then_inc(vs.sem, 1)

        a32.reset(m32)
        a16.reset(m16)

    def phase_a3(self):
        nc = self.nc
        a32, a16 = self.a32, self.a16
        m32, m16 = a32.mark(), a16.mark()
        xre = [a32.alloc(1024), a32.alloc(1024)]
        z1r = [a32.alloc(1024), a32.alloc(1024)]
        dub = [a32.alloc(1024), a32.alloc(1024)]
        x1a = [a32.alloc(1024), a32.alloc(1024)]
        x1g = [a32.alloc(1024), a32.alloc(1024)]
        z2s = [(a16.alloc(1024), a16.alloc(1024)) for _ in range(2)]
        sQ3 = self.tiny[:, 24:25]
        sSD3 = self.tiny[:, 25:26]
        sR3 = self.tiny[:, 26:27]

        ld, vs, ac, st, gp = self.ld, self.vs, self.ac, self.st, self.gp
        sch = []
        for tch in range(T // 128):
            it = {"tch": tch, "slot": tch % 2, "ld": ld.inc(16), "use": []}
            for dh in range(2):
                b = (tch % 2) * 2 + dh
                self.uses[b] += 1
                it["use"].append(self.uses[b])
            it["vs1"] = vs.inc(1)
            it["gp1"] = gp.inc(1)
            it["ac1"] = ac.inc(1)
            it["vs2"] = vs.inc(1)
            it["st"] = st.inc(48)
            sch.append(it)

        with nc.Block() as blk:
            @blk.sync
            def _(s):
                for it in sch:
                    tch, sl = it["tch"], it["slot"]
                    if tch >= 2:
                        s.wait_ge(vs.sem, sch[tch - 2]["vs1"])
                    s.dma_start(out=xre[sl][:, :],
                                in_=self.i_x[tch * 128:(tch + 1) * 128, :]
                                ).then_inc(ld.sem, 16)

            @blk.tensor
            def _(te):
                for it in sch:
                    tch = it["tch"]
                    t0 = tch * 128
                    for dh in range(2):
                        b = (tch % 2) * 2 + dh
                        use = it["use"][dh]
                        self.bank_acquire(te, b, use)
                        last = None
                        for nc2 in range(2):
                            hh = self.hsp_h[:, nc2 * T + t0:nc2 * T + t0 + 128]
                            hl = self.hsp_l[:, nc2 * T + t0:nc2 * T + t0 + 128]
                            ch = self.cw_h[:, nc2 * 1024 + dh * 512:
                                           nc2 * 1024 + dh * 512 + 512]
                            cl = self.cw_l[:, nc2 * 1024 + dh * 512:
                                           nc2 * 1024 + dh * 512 + 512]
                            te.ldweights(hh)
                            te.matmul(self.ps[b][:, :], hh, ch,
                                      start=(nc2 == 0), stop=False)
                            te.matmul(self.ps[b][:, :], hh, cl,
                                      start=False, stop=False)
                            te.ldweights(hl)
                            last = te.matmul(self.ps[b][:, :], hl, ch,
                                             start=False, stop=(nc2 == 1))
                        last.then_inc(self.ted[b].sem, 1)

            @blk.vector
            def _(v):
                for it in sch:
                    tch, sl = it["tch"], it["slot"]
                    c = tch
                    if tch >= 2:
                        v.wait_ge(st.sem, sch[tch - 2]["st"])
                    v.wait_ge(ld.sem, it["ld"])
                    v.tensor_scalar(z1r[sl][:, :], xre[sl][:, :],
                                    self.stM[:, c:c + 1], self.stR[:, c:c + 1],
                                    op0=ALU.subtract, op1=ALU.mult)
                    v.scalar_tensor_tensor(dub[sl][:, :], z1r[sl][:, :],
                                           self.stR2[:, c:c + 1], self.dws[:, :],
                                           op0=ALU.mult, op1=ALU.mult)
                    last = None
                    for dh in range(2):
                        b = (tch % 2) * 2 + dh
                        v.wait_ge(self.ted[b].sem, it["use"][dh])
                        last = v.tensor_tensor(
                            x1a[sl][:, dh * 512:(dh + 1) * 512],
                            self.ps[b][:, :],
                            z1r[sl][:, dh * 512:(dh + 1) * 512], op=ALU.add)
                        last.then_inc(self.psf[b].sem, 1)
                    v.nop()
                    v.drain(fusable=False).then_inc(vs.sem, 1)
                    v.wait_ge(ac.sem, it["ac1"])
                    v.reciprocal(sR3[:, :], sSD3[:, :])
                    v.drain(fusable=False)
                    zh, zl = z2s[sl]
                    v.tensor_scalar(zh[:, :], x1g[sl][:, :], sR3[:, :], None,
                                    op0=ALU.mult)
                    v.scalar_tensor_tensor(zl[:, :], x1g[sl][:, :], sR3[:, :],
                                           zh[:, :], op0=ALU.mult,
                                           op1=ALU.subtract)
                    v.drain(fusable=False).then_inc(vs.sem, 1)

            @blk.gpsimd
            def _(g):
                for it in sch:
                    tch, sl = it["tch"], it["slot"]
                    g.wait_ge(vs.sem, it["vs1"])
                    g.tensor_add(x1g[sl][:, :], x1a[sl][:, :], dub[sl][:, :]
                                 )
                    g.drain(fusable=False).then_inc(gp.sem, 1)
                    g.wait_ge(vs.sem, it["vs2"])
                    zh, zl = z2s[sl]
                    rows = slice(tch * 128, (tch + 1) * 128)
                    g.dma_start(out=self.s_x1[rows, :],
                                in_=x1g[sl][:, :]).then_inc(st.sem, 16)
                    g.dma_start(out=self.s_z2h[rows, :],
                                in_=zh[:, :]).then_inc(st.sem, 16)
                    g.dma_start(out=self.s_z2l[rows, :],
                                in_=zl[:, :]).then_inc(st.sem, 16)

            @blk.scalar
            def _(sc):
                for it in sch:
                    tch, sl = it["tch"], it["slot"]
                    sc.wait_ge(gp.sem, it["gp1"])
                    sc.activation(self.dump[:, :], x1g[sl][:, :], AF.Square,
                                  accum_out=sQ3[:, :])
                    sc.activation(sSD3[:, :], sQ3[:, :], AF.Sqrt,
                                  bias=self.cst[:, 1:2],
                                  scale=self.cst[:, 2:3])
                    sc.drain(fusable=False).then_inc(ac.sem, 1)

        a32.reset(m32)
        a16.reset(m16)

    def phase_b(self, q):
        nc = self.nc
        tiles = BQ[q]
        nj = len(tiles)
        widths = [FT_W[j] for j in tiles]
        offs = [FT_OFF[j] - FT_OFF[tiles[0]] for j in tiles]
        fq = sum(widths)
        c0w = FT_OFF[tiles[0]]
        a32, a16 = self.a32, self.a16
        m32, m16 = a32.mark(), a16.mark()
        w1h = a16.alloc(KC * fq)
        w1l = a16.alloc(KC * fq)
        w3h = a16.alloc(KC * fq)
        w3l = a16.alloc(KC * fq)
        w2h = a16.alloc(nj * 1024)
        w2l = a16.alloc(nj * 1024)
        zfh = [a16.alloc(KC * 512) for _ in range(2)]
        zfl = [a16.alloc(KC * 512) for _ in range(2)]
        gvh = a16.alloc(nj * 512)
        gvl = a16.alloc(nj * 512)
        if q == 3:
            z3h = [a16.alloc(1024), a16.alloc(1024)]
            z3l = [a16.alloc(1024), a16.alloc(1024)]
            sS4 = self.tiny[:, 32:33]
            sQ4 = self.tiny[:, 33:34]
            sV4 = self.tiny[:, 34:35]
            sSD4 = self.tiny[:, 35:36]
            sV4a = self.tiny[:, 38:39]
            sM4 = self.tiny[:, 36:37]
            sR4 = self.tiny[:, 37:38]
        mw32 = a32.mark()
        wst = [a32.alloc(1024), a32.alloc(1024)]

        wl, vs, ac, st, tr, pf, gp, tf = (self.wl, self.vs, self.ac, self.st,
                                          self.tr, self.pf, self.gp, self.tf)

        wsch = [{"wl": wl.inc(16), "vs": vs.inc(1)} for _ in range(2 * KC + nj)]
        with nc.Block() as blk:
            @blk.sync
            def _(s):
                idx = 0
                for hbm in (self.i_w1t, self.i_w3t):
                    for k in range(KC):
                        if idx >= 2:
                            s.wait_ge(vs.sem, wsch[idx - 2]["vs"])
                        s.dma_start(out=wst[idx % 2][:, 0:fq],
                                    in_=hbm[k * 128:(k + 1) * 128,
                                            c0w:c0w + fq]).then_inc(wl.sem, 16)
                        idx += 1
                for ji in range(nj):
                    if idx >= 2:
                        s.wait_ge(vs.sem, wsch[idx - 2]["vs"])
                    s.dma_start(
                        out=wst[idx % 2][0:widths[ji], :],
                        in_=self.i_w2t[c0w + offs[ji]:c0w + offs[ji] + widths[ji],
                                       :]).then_inc(wl.sem, 16)
                    idx += 1

            @blk.vector
            def _(v):
                idx = 0
                for dsth, dstl in ((w1h, w1l), (w3h, w3l)):
                    for k in range(KC):
                        v.wait_ge(wl.sem, wsch[idx]["wl"])
                        w = wst[idx % 2][:, 0:fq]
                        hi = dsth[:, k * fq:(k + 1) * fq]
                        lo = dstl[:, k * fq:(k + 1) * fq]
                        v.tensor_scalar(hi, w, self.ffw8[:, k:k + 1], None,
                                        op0=ALU.mult)
                        v.drain(fusable=False)
                        v.scalar_tensor_tensor(lo, w, self.ffw8[:, k:k + 1],
                                               hi, op0=ALU.mult,
                                               op1=ALU.subtract
                                               )
                        v.drain(fusable=False).then_inc(vs.sem, 1)
                        idx += 1
                for ji in range(nj):
                    v.wait_ge(wl.sem, wsch[idx]["wl"])
                    w = wst[idx % 2][0:widths[ji], :]
                    hi = w2h[0:widths[ji], ji * 1024:(ji + 1) * 1024]
                    lo = w2l[0:widths[ji], ji * 1024:(ji + 1) * 1024]
                    v.tensor_copy(hi, w)
                    v.drain(fusable=False)
                    v.tensor_sub(lo, w, hi)
                    v.drain(fusable=False).then_inc(vs.sem, 1)
                    idx += 1
        vs_w_end = vs.n
        a32.reset(mw32)
        sg = [a32.alloc(512), a32.alloc(512)]
        gv32 = [a32.alloc(512), a32.alloc(512)]
        pin = [a32.alloc(512), a32.alloc(512)]
        pout = [a32.alloc(512), a32.alloc(512)]
        if q == 3:
            x1r = [a32.alloc(512), a32.alloc(512)]
            xr = [a32.alloc(512), a32.alloc(512)]
            tbuf = [a32.alloc(512), a32.alloc(512)]
            x2b = [a32.alloc(1024) for _ in range(4)]
            z3f = [a32.alloc(1024), a32.alloc(1024)]

        st0 = self.st.n
        sch = []
        wb_flat = []
        z3_flat = []
        jj_flat = []
        for tg in range(NTGB):
            it = {"tg": tg, "slot": tg % 2, "tr": tr.inc(16 * 16), "jj": []}
            for ji in range(nj):
                grp = ji % 2
                bg, bv = 2 * grp, 2 * grp + 1
                self.uses[bg] += 1
                self.uses[bv] += 1
                jit = {"ji": ji, "bg": bg, "bv": bv,
                       "ug": self.uses[bg], "uv": self.uses[bv],
                       "ac": ac.inc(1), "vs": vs.inc(1)}
                it["jj"].append(jit)
                jj_flat.append(jit)
            it["gv_done"] = vs.n
            it["wb"] = []
            for tc in range(TGB // 128):
                for dh in range(2):
                    b = 4 + (tc * 2 + dh) % 4
                    self.uses[b] += 1
                    wit = {"tg": tg, "tc": tc, "dh": dh, "b": b,
                           "u": (tc * 2 + dh) % 2, "use": self.uses[b],
                           "pf": None, "st": None}
                    if q > 0:
                        wit["pf"] = pf.inc(16)
                    if q == 3:
                        wit["pf3"] = pf.inc(32)
                    wit["vs"] = vs.inc(1)
                    if q < 3:
                        wit["st"] = st.inc(16)
                    it["wb"].append(wit)
                    wb_flat.append(wit)
            it["tf"] = tf.inc(1)
            if q == 3:
                it["z3"] = []
                for tc in range(TGB // 128):
                    zit = {"tg": tg, "tc": tc,
                           "ac1": ac.inc(1), "vs0": vs.inc(1),
                           "ac2": ac.inc(1), "vs": vs.inc(1),
                           "st": st.inc(48)}
                    it["z3"].append(zit)
                    z3_flat.append(zit)
            sch.append(it)

        with nc.Block() as blk:
            @blk.sync
            def _(s):
                if st0:
                    s.wait_ge(self.st.sem, st0)
                for it in sch:
                    tg, sl = it["tg"], it["slot"]
                    t0 = tg * TGB
                    if tg >= 2:
                        s.wait_ge(tf.sem, sch[tg - 2]["tf"])
                    for k in range(KC):
                        s.dma_start(out=zfh[sl][:, k * 512:(k + 1) * 512],
                                    in_=self.s_z2h[t0:t0 + TGB,
                                                   k * 128:(k + 1) * 128],
                                    transpose=True).then_inc(tr.sem, 16)
                        s.dma_start(out=zfl[sl][:, k * 512:(k + 1) * 512],
                                    in_=self.s_z2l[t0:t0 + TGB,
                                                   k * 128:(k + 1) * 128],
                                    transpose=True).then_inc(tr.sem, 16)
                    for wit in it["wb"]:
                        tc, dh, u = wit["tc"], wit["dh"], wit["u"]
                        gi = wb_flat.index(wit)
                        rows = slice(t0 + tc * 128, t0 + (tc + 1) * 128)
                        cols = slice(dh * 512, (dh + 1) * 512)
                        if gi >= 2 and (q > 0 or q == 3):
                            s.wait_ge(vs.sem, wb_flat[gi - 2]["vs"])
                        if q > 0:
                            s.dma_start(out=pin[u][:, :],
                                        in_=self.s_pb[rows, cols]
                                        ).then_inc(pf.sem, 16)
                        if q == 3:
                            s.dma_start(out=x1r[u][:, :],
                                        in_=self.s_x1[rows, cols]
                                        ).then_inc(pf.sem, 16)
                            s.dma_start(out=xr[u][:, :],
                                        in_=self.i_x[rows, cols]
                                        ).then_inc(pf.sem, 16)

            @blk.tensor
            def _(te):
                te.wait_ge(vs.sem, vs_w_end)
                for it in sch:
                    tg, sl = it["tg"], it["slot"]
                    te.wait_ge(tr.sem, it["tr"])
                    for jit in it["jj"]:
                        ji = jit["ji"]
                        wj = widths[ji]
                        for b, wh_, wl_, use in ((jit["bg"], w1h, w1l, jit["ug"]),
                                                 (jit["bv"], w3h, w3l, jit["uv"])):
                            self.bank_acquire(te, b, use)
                            last = None
                            for k in range(KC):
                                co = k * fq + offs[ji]
                                wh = wh_[:, co:co + wj]
                                wlo = wl_[:, co:co + wj]
                                xh = zfh[sl][:, k * 512:(k + 1) * 512]
                                xl = zfl[sl][:, k * 512:(k + 1) * 512]
                                te.ldweights(wh)
                                te.matmul(self.ps[b][0:wj, :], wh, xh,
                                          start=(k == 0), stop=False)
                                te.matmul(self.ps[b][0:wj, :], wh, xl,
                                          start=False, stop=False)
                                te.ldweights(wlo)
                                last = te.matmul(self.ps[b][0:wj, :], wlo, xh,
                                                 start=False,
                                                 stop=(k == KC - 1))
                            last.then_inc(self.ted[b].sem, 1)
                    te.wait_ge(vs.sem, it["gv_done"])
                    for wit in it["wb"]:
                        tc, dh, b, use = wit["tc"], wit["dh"], wit["b"], wit["use"]
                        self.bank_acquire(te, b, use)
                        last = None
                        for ji in range(nj):
                            wj = widths[ji]
                            gh = gvh[0:wj, ji * 512 + tc * 128:
                                     ji * 512 + tc * 128 + 128]
                            gl = gvl[0:wj, ji * 512 + tc * 128:
                                     ji * 512 + tc * 128 + 128]
                            rh = w2h[0:wj, ji * 1024 + dh * 512:
                                     ji * 1024 + dh * 512 + 512]
                            rl = w2l[0:wj, ji * 1024 + dh * 512:
                                     ji * 1024 + dh * 512 + 512]
                            te.ldweights(gh)
                            te.matmul(self.ps[b][:, :], gh, rh,
                                      start=(ji == 0), stop=False)
                            te.matmul(self.ps[b][:, :], gh, rl,
                                      start=False, stop=False)
                            te.ldweights(gl)
                            last = te.matmul(self.ps[b][:, :], gl, rh,
                                             start=False, stop=(ji == nj - 1))
                        last.then_inc(self.ted[b].sem, 1)
                    te.nop().then_inc(tf.sem, 1)

            @blk.scalar
            def _(sc):
                for it in sch:
                    for jit in it["jj"]:
                        ji, bg = jit["ji"], jit["bg"]
                        wj = widths[ji]
                        u = ji % 2
                        gj = jj_flat.index(jit)
                        if gj >= 2:
                            sc.wait_ge(vs.sem, jj_flat[gj - 2]["vs"])
                        sc.wait_ge(self.ted[bg].sem, jit["ug"])
                        sc.activation(sg[u][0:wj, :], self.ps[bg][0:wj, :],
                                      AF.Silu)
                        sc.drain(fusable=False).then_inc(ac.sem, 1)
                    if q == 3:
                        for zit in it["z3"]:
                            tc = zit["tc"]
                            gi = z3_flat.index(zit)
                            if gi >= 1:
                                sc.wait_ge(vs.sem, z3_flat[gi - 1]["vs0"])
                            sc.wait_ge(vs.sem, it["wb"][tc * 2 + 1]["vs"])
                            sc.activation(self.dump[:, :], x2b[tc][:, :],
                                          AF.Identity, accum_out=sS4[:, :])
                            sc.activation(self.dump[:, :], x2b[tc][:, :],
                                          AF.Square, accum_out=sQ4[:, :])
                            sc.drain(fusable=False).then_inc(ac.sem, 1)
                            sc.wait_ge(vs.sem, zit["vs0"])
                            sc.activation(sSD4[:, :], sV4[:, :], AF.Sqrt,
                                          bias=self.cst[:, 0:1])
                            sc.drain(fusable=False).then_inc(ac.sem, 1)

            @blk.vector
            def _(v):
                for it in sch:
                    tg, sl = it["tg"], it["slot"]
                    if tg >= 1:
                        v.wait_ge(tf.sem, sch[tg - 1]["tf"])
                    for jit in it["jj"]:
                        ji, bg, bv_ = jit["ji"], jit["bg"], jit["bv"]
                        wj = widths[ji]
                        u = ji % 2
                        v.wait_ge(ac.sem, jit["ac"])
                        v.wait_ge(self.ted[bv_].sem, jit["uv"])
                        gh = gvh[0:wj, ji * 512:(ji + 1) * 512]
                        gl = gvl[0:wj, ji * 512:(ji + 1) * 512]
                        v.tensor_tensor(gh, sg[u][0:wj, :],
                                        self.ps[bv_][0:wj, :], op=ALU.mult
                                        ).then_inc(self.psf[bg].sem, 1)
                        v.tensor_tensor(gv32[u][0:wj, :], sg[u][0:wj, :],
                                        self.ps[bv_][0:wj, :], op=ALU.mult
                                        ).then_inc(self.psf[bv_].sem, 1)
                        v.tensor_sub(gl, gv32[u][0:wj, :], gh
                                     )
                        v.drain(fusable=False).then_inc(vs.sem, 1)
                    for wit in it["wb"]:
                        tc, dh, b, use = wit["tc"], wit["dh"], wit["b"], wit["use"]
                        u = wit["u"]
                        gi = wb_flat.index(wit)
                        v.wait_ge(self.ted[b].sem, use)
                        if q > 0:
                            v.wait_ge(pf.sem, wit["pf"])
                        if q == 3:
                            v.wait_ge(pf.sem, wit["pf3"])
                        if q < 3 and gi >= 2:
                            v.wait_ge(st.sem, wb_flat[gi - 2]["st"])
                        if q == 0:
                            v.tensor_copy(pout[u][:, :], self.ps[b][:, :]
                                          ).then_inc(self.psf[b].sem, 1)
                            v.nop()
                            v.drain(fusable=False).then_inc(vs.sem, 1)
                        elif q < 3:
                            i = v.tensor_tensor(pout[u][:, :], self.ps[b][:, :],
                                                pin[u][:, :], op=ALU.add)
                            i.then_inc(self.psf[b].sem, 1)
                            v.nop()
                            v.drain(fusable=False).then_inc(vs.sem, 1)
                        else:
                            idx3 = tg * 4 + tc
                            if idx3 >= 4:
                                v.wait_ge(st.sem, z3_flat[idx3 - 4]["st"])
                            v.tensor_tensor(pout[u][:, :], self.ps[b][:, :],
                                            pin[u][:, :], op=ALU.add
                                            ).then_inc(self.psf[b].sem, 1)
                            v.tensor_add(tbuf[u][:, :], pout[u][:, :],
                                         x1r[u][:, :])
                            x2half = x2b[tc][:, dh * 512:(dh + 1) * 512]
                            v.tensor_add(x2half, tbuf[u][:, :], xr[u][:, :]
                                         )
                            v.drain(fusable=False).then_inc(vs.sem, 1)
                    if q == 3:
                        for zit in it["z3"]:
                            tc = zit["tc"]
                            u = tc % 2
                            idx3 = tg * 4 + tc
                            if idx3 >= 2:
                                v.wait_ge(st.sem, z3_flat[idx3 - 2]["st"])
                            v.wait_ge(ac.sem, zit["ac1"])
                            v.tensor_single_scalar(sM4[:, :], sS4[:, :],
                                                   1.0 / D, op=ALU.mult)
                            v.drain(fusable=False)
                            v.tensor_mul(sV4a[:, :], sM4[:, :], sM4[:, :])
                            v.drain(fusable=False)
                            v.scalar_tensor_tensor(sV4[:, :], sQ4[:, :],
                                                   1.0 / D, sV4a[:, :],
                                                   op0=ALU.mult,
                                                   op1=ALU.subtract
                                                   )
                            v.drain(fusable=False).then_inc(vs.sem, 1)
                            v.wait_ge(ac.sem, zit["ac2"])
                            v.reciprocal(sR4[:, :], sSD4[:, :])
                            v.drain(fusable=False)
                            v.tensor_scalar(z3f[u][:, :], x2b[tc][:, :],
                                            sM4[:, :], sR4[:, :],
                                            op0=ALU.subtract, op1=ALU.mult)
                            v.tensor_copy(z3h[u][:, :], z3f[u][:, :])
                            v.tensor_sub(z3l[u][:, :], z3f[u][:, :],
                                         z3h[u][:, :])
                            v.drain(fusable=False).then_inc(vs.sem, 1)

            @blk.gpsimd
            def _(g):
                for it in sch:
                    tg = it["tg"]
                    t0 = tg * TGB
                    if q < 3:
                        for wit in it["wb"]:
                            tc, dh, u = wit["tc"], wit["dh"], wit["u"]
                            g.wait_ge(vs.sem, wit["vs"])
                            rows = slice(t0 + tc * 128, t0 + (tc + 1) * 128)
                            cols = slice(dh * 512, (dh + 1) * 512)
                            g.dma_start(out=self.s_pb[rows, cols],
                                        in_=pout[u][:, :]).then_inc(st.sem, 16)
                    else:
                        for zit in it["z3"]:
                            tc = zit["tc"]
                            u = tc % 2
                            g.wait_ge(vs.sem, zit["vs"])
                            rows = slice(t0 + tc * 128, t0 + (tc + 1) * 128)
                            g.dma_start(out=self.s_x2[rows, :],
                                        in_=x2b[tc][:, :]).then_inc(st.sem, 16)
                            g.dma_start(out=self.s_z3h[rows, :],
                                        in_=z3h[u][:, :]).then_inc(st.sem, 16)
                            g.dma_start(out=self.s_z3l[rows, :],
                                        in_=z3l[u][:, :]).then_inc(st.sem, 16)

        a32.reset(m32)
        a16.reset(m16)

    def phase_c(self, q):
        nc = self.nc
        a32, a16 = self.a32, self.a16
        m32, m16 = a32.mark(), a16.mark()
        fch = a16.alloc(KC * 1024)
        fcl = a16.alloc(KC * 1024)
        pjh = a16.alloc(8 * 1024)
        pjl = a16.alloc(8 * 1024)
        zfh = [a16.alloc(KC * 512) for _ in range(2)]
        zfl = [a16.alloc(KC * 512) for _ in range(2)]
        hp_h = a16.alloc(8 * 512)
        hp_l = a16.alloc(8 * 512)
        mw32 = a32.mark()
        wst = [a32.alloc(1024), a32.alloc(1024)]

        wl, vs, ac, st, tr, pf, gp, tf = (self.wl, self.vs, self.ac, self.st,
                                          self.tr, self.pf, self.gp, self.tf)
        f0 = CQ[q][0] * 128

        wsch = [{"wl": wl.inc(16), "vs": vs.inc(1)} for _ in range(KC + 8)]
        with nc.Block() as blk:
            @blk.sync
            def _(s):
                idx = 0
                for k in range(KC):
                    if idx >= 2:
                        s.wait_ge(vs.sem, wsch[idx - 2]["vs"])
                    s.dma_start(out=wst[idx % 2][:, :],
                                in_=self.i_fct[k * 128:(k + 1) * 128,
                                               f0:f0 + 1024]
                                ).then_inc(wl.sem, 16)
                    idx += 1
                for j in range(8):
                    if idx >= 2:
                        s.wait_ge(vs.sem, wsch[idx - 2]["vs"])
                    s.dma_start(out=wst[idx % 2][:, :],
                                in_=self.i_pjt[f0 + j * 128:f0 + (j + 1) * 128,
                                               :]).then_inc(wl.sem, 16)
                    idx += 1

            @blk.vector
            def _(v):
                idx = 0
                for k in range(KC):
                    v.wait_ge(wl.sem, wsch[idx]["wl"])
                    w = wst[idx % 2][:, :]
                    hi = fch[:, k * 1024:(k + 1) * 1024]
                    lo = fcl[:, k * 1024:(k + 1) * 1024]
                    v.tensor_copy(hi, w)
                    v.drain(fusable=False)
                    v.tensor_sub(lo, w, hi)
                    v.drain(fusable=False).then_inc(vs.sem, 1)
                    idx += 1
                for j in range(8):
                    v.wait_ge(wl.sem, wsch[idx]["wl"])
                    w = wst[idx % 2][:, :]
                    hi = pjh[:, j * 1024:(j + 1) * 1024]
                    lo = pjl[:, j * 1024:(j + 1) * 1024]
                    v.tensor_copy(hi, w)
                    v.drain(fusable=False)
                    v.tensor_sub(lo, w, hi)
                    v.drain(fusable=False).then_inc(vs.sem, 1)
                    idx += 1
        vs_w_end = vs.n
        a32.reset(mw32)
        r32 = [a32.alloc(512), a32.alloc(512)]
        h32 = [a32.alloc(512), a32.alloc(512)]
        pin = [a32.alloc(512), a32.alloc(512)]
        pout = [a32.alloc(512), a32.alloc(512)]
        if q == 3:
            x2r = [a32.alloc(512), a32.alloc(512)]
            y2 = [a32.alloc(512), a32.alloc(512)]
            inds = [a32.alloc(512) for _ in range(5)]
            gA = a32.alloc(512)
            gB = a32.alloc(512)
            t1b = a32.alloc(512)
            t2b = a32.alloc(512)
            ob = [a32.alloc(512), a32.alloc(512)]

        st0 = self.st.n
        sch = []
        wb_flat = []
        jj_flat = []
        for tg in range(NTGC):
            it = {"tg": tg, "slot": tg % 2, "tr": tr.inc(16 * 16), "jj": []}
            for j in range(8):
                b = j % 4
                self.uses[b] += 1
                jit = {"j": j, "b": b, "use": self.uses[b],
                       "ac": ac.inc(1), "vs": vs.inc(1), "gp": gp.inc(1)}
                it["jj"].append(jit)
                jj_flat.append(jit)
            it["h_done_vs"] = vs.n
            it["h_done_gp"] = gp.n
            it["wb"] = []
            for tc in range(TGC // 128):
                for dh in range(2):
                    b = 4 + (tc * 2 + dh) % 4
                    self.uses[b] += 1
                    wit = {"tg": tg, "tc": tc, "dh": dh, "b": b,
                           "u": (tc * 2 + dh) % 2, "use": self.uses[b]}
                    if q > 0:
                        wit["pf"] = pf.inc(16)
                    if q == 3:
                        wit["pfx"] = pf.inc(16)
                    wit["vs"] = vs.inc(1)
                    if q == 3:
                        wit["vsi"] = vs.inc(1)
                        wit["gp"] = gp.inc(1)
                        wit["vsm"] = vs.inc(1)
                    wit["st"] = st.inc(16)
                    it["wb"].append(wit)
                    wb_flat.append(wit)
            it["tf"] = tf.inc(1)
            sch.append(it)

        with nc.Block() as blk:
            @blk.sync
            def _(s):
                if st0:
                    s.wait_ge(self.st.sem, st0)
                for it in sch:
                    tg, sl = it["tg"], it["slot"]
                    t0 = tg * TGC
                    if tg >= 2:
                        s.wait_ge(tf.sem, sch[tg - 2]["tf"])
                    for k in range(KC):
                        s.dma_start(out=zfh[sl][:, k * 512:(k + 1) * 512],
                                    in_=self.s_z3h[t0:t0 + TGC,
                                                   k * 128:(k + 1) * 128],
                                    transpose=True).then_inc(tr.sem, 16)
                        s.dma_start(out=zfl[sl][:, k * 512:(k + 1) * 512],
                                    in_=self.s_z3l[t0:t0 + TGC,
                                                   k * 128:(k + 1) * 128],
                                    transpose=True).then_inc(tr.sem, 16)
                    for wit in it["wb"]:
                        tc, dh, u = wit["tc"], wit["dh"], wit["u"]
                        gi = wb_flat.index(wit)
                        rows = slice(t0 + tc * 128, t0 + (tc + 1) * 128)
                        cols = slice(dh * 512, (dh + 1) * 512)
                        if gi >= 2 and (q > 0 or q == 3):
                            prev = wb_flat[gi - 2]
                            s.wait_ge(vs.sem, prev.get("vsm") or prev["vs"])
                        if q > 0:
                            s.dma_start(out=pin[u][:, :],
                                        in_=self.s_yp[rows, cols]
                                        ).then_inc(pf.sem, 16)
                        if q == 3:
                            s.dma_start(out=x2r[u][:, :],
                                        in_=self.s_x2[rows, cols]
                                        ).then_inc(pf.sem, 16)

            @blk.tensor
            def _(te):
                te.wait_ge(vs.sem, vs_w_end)
                for it in sch:
                    tg, sl = it["tg"], it["slot"]
                    te.wait_ge(tr.sem, it["tr"])
                    for jit in it["jj"]:
                        j, b, use = jit["j"], jit["b"], jit["use"]
                        self.bank_acquire(te, b, use)
                        last = None
                        for k in range(KC):
                            wh = fch[:, k * 1024 + j * 128:
                                     k * 1024 + j * 128 + 128]
                            wlo = fcl[:, k * 1024 + j * 128:
                                      k * 1024 + j * 128 + 128]
                            xh = zfh[sl][:, k * 512:(k + 1) * 512]
                            xl = zfl[sl][:, k * 512:(k + 1) * 512]
                            te.ldweights(wh)
                            te.matmul(self.ps[b][:, :], wh, xh,
                                      start=(k == 0), stop=False)
                            te.matmul(self.ps[b][:, :], wh, xl,
                                      start=False, stop=False)
                            te.ldweights(wlo)
                            last = te.matmul(self.ps[b][:, :], wlo, xh,
                                             start=False, stop=(k == KC - 1))
                        last.then_inc(self.ted[b].sem, 1)
                    te.wait_ge(vs.sem, it["h_done_vs"])
                    te.wait_ge(gp.sem, it["h_done_gp"])
                    for wit in it["wb"]:
                        tc, dh, b, use = wit["tc"], wit["dh"], wit["b"], wit["use"]
                        self.bank_acquire(te, b, use)
                        last = None
                        for j in range(8):
                            hh = hp_h[:, j * 512 + tc * 128:
                                      j * 512 + tc * 128 + 128]
                            hl = hp_l[:, j * 512 + tc * 128:
                                      j * 512 + tc * 128 + 128]
                            rh = pjh[:, j * 1024 + dh * 512:
                                     j * 1024 + dh * 512 + 512]
                            rl = pjl[:, j * 1024 + dh * 512:
                                     j * 1024 + dh * 512 + 512]
                            te.ldweights(hh)
                            te.matmul(self.ps[b][:, :], hh, rh,
                                      start=(j == 0), stop=False)
                            te.matmul(self.ps[b][:, :], hh, rl,
                                      start=False, stop=False)
                            te.ldweights(hl)
                            last = te.matmul(self.ps[b][:, :], hl, rh,
                                             start=False, stop=(j == 7))
                        last.then_inc(self.ted[b].sem, 1)
                    te.nop().then_inc(tf.sem, 1)

            @blk.scalar
            def _(sc):
                for it in sch:
                    for jit in it["jj"]:
                        j, b, use = jit["j"], jit["b"], jit["use"]
                        u = j % 2
                        gj = jj_flat.index(jit)
                        if gj >= 2:
                            sc.wait_ge(vs.sem, jj_flat[gj - 2]["vs"])
                        sc.wait_ge(self.ted[b].sem, use)
                        sc.activation(r32[u][:, :], self.ps[b][:, :], AF.Relu
                                      )
                        sc.drain(fusable=False).then_inc(ac.sem, 1)

            @blk.vector
            def _(v):
                for it in sch:
                    tg, sl = it["tg"], it["slot"]
                    if tg >= 1:
                        v.wait_ge(tf.sem, sch[tg - 1]["tf"])
                    for jit in it["jj"]:
                        j, b, use = jit["j"], jit["b"], jit["use"]
                        u = j % 2
                        v.wait_ge(ac.sem, jit["ac"])
                        gj = jj_flat.index(jit)
                        if gj >= 2:
                            v.wait_ge(gp.sem, jj_flat[gj - 2]["gp"])
                        hh = hp_h[:, j * 512:(j + 1) * 512]
                        v.tensor_tensor(hh, r32[u][:, :], self.ps[b][:, :],
                                        op=ALU.mult)
                        v.tensor_tensor(h32[u][:, :], r32[u][:, :],
                                        self.ps[b][:, :], op=ALU.mult
                                        ).then_inc(self.psf[b].sem, 1)
                        v.nop()
                        v.drain(fusable=False).then_inc(vs.sem, 1)
                    for wit in it["wb"]:
                        tc, dh, b, use = wit["tc"], wit["dh"], wit["b"], wit["use"]
                        u = wit["u"]
                        gi = wb_flat.index(wit)
                        v.wait_ge(self.ted[b].sem, use)
                        if q > 0:
                            v.wait_ge(pf.sem, wit["pf"])
                        if gi >= 2:
                            v.wait_ge(st.sem, wb_flat[gi - 2]["st"])
                        if q == 0:
                            v.tensor_copy(pout[u][:, :], self.ps[b][:, :]
                                          ).then_inc(self.psf[b].sem, 1)
                            v.nop()
                            v.drain(fusable=False).then_inc(vs.sem, 1)
                        elif q < 3:
                            v.tensor_tensor(pout[u][:, :], self.ps[b][:, :],
                                            pin[u][:, :], op=ALU.add
                                            ).then_inc(self.psf[b].sem, 1)
                            v.nop()
                            v.drain(fusable=False).then_inc(vs.sem, 1)
                        else:
                            v.tensor_tensor(y2[u][:, :], self.ps[b][:, :],
                                            pin[u][:, :], op=ALU.add
                                            ).then_inc(self.psf[b].sem, 1)
                            v.nop()
                            v.drain(fusable=False).then_inc(vs.sem, 1)
                            for bi, (bh, blo) in enumerate(_BPS):
                                i = v.tensor_scalar(inds[bi][:, :], y2[u][:, :],
                                                    bh, blo, op0=ALU.subtract,
                                                    op1=ALU.is_gt)
                            v.drain(fusable=False).then_inc(vs.sem, 1)
                            v.wait_ge(gp.sem, wit["gp"])
                            v.wait_ge(pf.sem, wit["pfx"])
                            v.tensor_add(t1b[:, :], gA[:, :], gB[:, :])
                            v.tensor_add(t2b[:, :], t1b[:, :], inds[4][:, :])
                            v.scalar_tensor_tensor(ob[u][:, :], t2b[:, :], 0.2,
                                                   x2r[u][:, :], op0=ALU.mult,
                                                   op1=ALU.add
                                                   )
                            v.drain(fusable=False).then_inc(vs.sem, 1)

            @blk.gpsimd
            def _(g):
                for it in sch:
                    tg = it["tg"]
                    t0 = tg * TGC
                    if tg >= 1:
                        g.wait_ge(tf.sem, sch[tg - 1]["tf"])
                    for jit in it["jj"]:
                        j = jit["j"]
                        u = j % 2
                        g.wait_ge(vs.sem, jit["vs"])
                        hl = hp_l[:, j * 512:(j + 1) * 512]
                        hh = hp_h[:, j * 512:(j + 1) * 512]
                        g.tensor_sub(hl, h32[u][:, :], hh)
                        g.drain(fusable=False).then_inc(gp.sem, 1)
                    for wit in it["wb"]:
                        tc, dh, u = wit["tc"], wit["dh"], wit["u"]
                        rows = slice(t0 + tc * 128, t0 + (tc + 1) * 128)
                        cols = slice(dh * 512, (dh + 1) * 512)
                        if q < 3:
                            g.wait_ge(vs.sem, wit["vs"])
                            g.dma_start(out=self.s_yp[rows, cols],
                                        in_=pout[u][:, :]).then_inc(st.sem, 16)
                        else:
                            g.wait_ge(vs.sem, wit["vsi"])
                            g.tensor_add(gA[:, :], inds[0][:, :], inds[1][:, :])
                            g.tensor_add(gB[:, :], inds[2][:, :], inds[3][:, :]
                                         )
                            g.drain(fusable=False).then_inc(gp.sem, 1)
                            g.wait_ge(vs.sem, wit["vsm"])
                            g.dma_start(out=self.o_out[rows, cols],
                                        in_=ob[u][:, :]).then_inc(st.sem, 16)

        a32.reset(m32)
        a16.reset(m16)

    def build(self, upto=99):
        p16_0 = self.a16.mark()
        self.hsp_h = self.a16.alloc(2 * T)
        self.hsp_l = self.a16.alloc(2 * T)

        self.phase_a_init()
        self.barrier()
        if upto >= 1:
            self.phase_a1(src=0)
            self.barrier()
        if upto >= 2:
            self.phase_a2(src=0)
            self.barrier()
        if upto >= 3:
            self.phase_a1(src=1)
            self.barrier()
        if upto >= 4:
            self.phase_a2(src=1)
            self.barrier()
        if upto >= 5:
            self.phase_a3()
            self.barrier()
        self.a16.reset(p16_0)
        for q in range(4):
            if upto >= 6 + q:
                self.phase_b(q)
                self.barrier()
        for q in range(4):
            if upto >= 10 + q:
                self.phase_c(q)
                self.barrier()
        if getattr(self, "dbg_stats", False):
            o_dbg = self.nc.dram_tensor("o_dbg", [128, 160], F32,
                                        kind="ExternalOutput").ap()
            with self.nc.Block() as blk:
                @blk.gpsimd
                def _(g):
                    g.dma_start(out=o_dbg[:, 0:64], in_=self.tiny[:, :]
                                ).then_inc(self.st.sem, 16)
                    g.dma_start(out=o_dbg[:, 64:96], in_=self.stM[:, :]
                                ).then_inc(self.st.sem, 16)
                    g.dma_start(out=o_dbg[:, 96:128], in_=self.stR[:, :]
                                ).then_inc(self.st.sem, 16)
                    g.dma_start(out=o_dbg[:, 128:160], in_=self.stR2[:, :]
                                ).then_inc(self.st.sem, 16)
                self.st.inc(64)
        with self.nc.Block() as blk:
            @blk.gpsimd
            def _(g):
                g.wait_ge(self.st.sem, self.st.n)
        self.stack.close()
        return self.nc


def _build():
    return Builder().build()


def kernel(L0_attn_x, L0_attn_log_lambda, L0_attn_B_w, L0_attn_C_w,
           L0_attn_D_skip, L0_attn_ssm_norm_w, L0_attn_ffn_norm_w,
           L0_attn_w1, L0_attn_w2, L0_attn_w3, L0_ffn_W_fc, L0_ffn_W_proj):
    x = np.ascontiguousarray(np.asarray(L0_attn_x, dtype=np.float32))
    B, S, _ = x.shape
    assert (B, S) == (4, 8192)
    tw = lambda a: np.ascontiguousarray(np.asarray(a, dtype=np.float32).T)
    com = {
        "BwT": tw(L0_attn_B_w), "CwT": tw(L0_attn_C_w),
        "w1T": tw(L0_attn_w1), "w3T": tw(L0_attn_w3), "w2T": tw(L0_attn_w2),
        "fcT": tw(L0_ffn_W_fc), "projT": tw(L0_ffn_W_proj),
        "log_lambda": np.ascontiguousarray(np.asarray(L0_attn_log_lambda, np.float32)),
        "D_skip": np.ascontiguousarray(np.asarray(L0_attn_D_skip, np.float32)),
        "ssm_w": np.ascontiguousarray(np.asarray(L0_attn_ssm_norm_w, np.float32)),
        "ffn_w": np.ascontiguousarray(np.asarray(L0_attn_ffn_norm_w, np.float32)),
    }
    zeros = np.zeros((T, D), dtype=np.float32)
    in_maps = []
    for c in range(8):
        seq, half = c // 2, c % 2
        m = dict(com)
        m["x"] = np.ascontiguousarray(x[seq, half * T:(half + 1) * T])
        m["xprev"] = np.ascontiguousarray(x[seq, 0:T]) if half else zeros
        in_maps.append(m)
    nc = _build()
    res = run_bass_kernel_spmd(nc, in_maps, core_ids=list(range(8)))
    out = np.empty((4, 8192, 1024), dtype=np.float32)
    for c in range(8):
        seq, half = c // 2, c % 2
        out[seq, half * T:(half + 1) * T] = res.results[c]["out"]
    return out


# version 06 (final)
# speedup vs baseline: 1.1136x; 1.1136x over previous
import numpy as np
from contextlib import ExitStack

import concourse.bass as bass
import concourse.mybir as mybir
from concourse.bass_utils import run_bass_kernel_spmd

F32 = mybir.dt.float32
F16 = mybir.dt.float16
AF = mybir.ActivationFunctionType
ALU = mybir.AluOpType

D = 1024
KC = 8
NS = 256
T = 4096
DF = 2736
DH = 4096
TS = 512
NTS = T // TS
EPS_LN = 1e-5
EPS_RMS = 1e-6

FT_W = [128] * 21 + [48]
FT_OFF = [sum(FT_W[:i]) for i in range(len(FT_W))]
BQ = [list(range(0, 6)), list(range(6, 12)), list(range(12, 17)), list(range(17, 22))]
CQ = [list(range(8 * q, 8 * q + 8)) for q in range(4)]
TGB = 512
NTGB = T // TGB
TGC = 512
NTGC = T // TGC


def _membrane_breakpoints():
    def g(y):
        y = np.asarray(y, dtype=np.float64)
        mem = np.zeros_like(y)
        out = np.zeros_like(y)
        for _ in range(5):
            reset = (mem > 1.0).astype(np.float64)
            mem = 0.9 * mem + y - reset
            out += (mem > 1.0)
        return out

    ys = np.linspace(-1.0, 2.0, 600001)
    vals = g(ys)
    jumps = np.nonzero(np.diff(vals) != 0)[0]
    assert len(jumps) == 5, len(jumps)
    out = []
    for j in jumps:
        lo, hi = float(ys[j]), float(ys[j + 1])
        vlo = g(np.array([lo]))[0]
        for _ in range(90):
            mid = 0.5 * (lo + hi)
            if g(np.array([mid]))[0] == vlo:
                lo = mid
            else:
                hi = mid
        b_hi = np.float32(lo)
        b_lo = np.float32(np.float64(lo) - np.float64(b_hi))
        out.append((float(b_hi), float(b_lo)))
    return out


_BPS = _membrane_breakpoints()


class Ctr:
    def __init__(self, stack, nc, name):
        self.sem = stack.enter_context(nc.semaphore(name))
        self.n = 0

    def inc(self, k=1):
        self.n += k
        return self.n


class Arena:
    def __init__(self, t, cols):
        self.t = t
        self.cols = cols
        self.off = 0

    def alloc(self, cols):
        assert self.off + cols <= self.cols, (self.off, cols, self.cols)
        ap = self.t[:, self.off:self.off + cols]
        self.off += cols
        return ap

    def mark(self):
        return self.off

    def reset(self, m):
        self.off = m


class Builder:
    def __init__(self, dbg=False):
        import os
        nc = bass.Bass("TRN2", target_bir_lowering=False,
                       detect_race_conditions=bool(os.environ.get("K_RD")))
        self.nc = nc
        dt = nc.dram_tensor
        self.i_x = dt("x", [T, D], F32, kind="ExternalInput").ap()
        self.i_xp = dt("xprev", [T, D], F32, kind="ExternalInput").ap()
        self.i_bwt = dt("BwT", [D, NS], F32, kind="ExternalInput").ap()
        self.i_cwt = dt("CwT", [NS, D], F32, kind="ExternalInput").ap()
        self.i_w1t = dt("w1T", [D, DF], F32, kind="ExternalInput").ap()
        self.i_w3t = dt("w3T", [D, DF], F32, kind="ExternalInput").ap()
        self.i_w2t = dt("w2T", [DF, D], F32, kind="ExternalInput").ap()
        self.i_fct = dt("fcT", [D, DH], F32, kind="ExternalInput").ap()
        self.i_pjt = dt("projT", [DH, D], F32, kind="ExternalInput").ap()
        self.i_ll = dt("log_lambda", [NS], F32, kind="ExternalInput").ap()
        self.i_dsk = dt("D_skip", [D], F32, kind="ExternalInput").ap()
        self.i_wsm = dt("ssm_w", [D], F32, kind="ExternalInput").ap()
        self.i_wff = dt("ffn_w", [D], F32, kind="ExternalInput").ap()
        self.o_out = dt("out", [T, D], F32, kind="ExternalOutput").ap()
        kd = "ExternalOutput" if dbg else "Internal"
        self.s_uh = dt("s_uh", [T, D], F16, kind=kd).ap()
        self.s_ul = dt("s_ul", [T, D], F16, kind=kd).ap()
        self.s_uhp = dt("s_uhp", [T, D], F16, kind=kd).ap()
        self.s_ulp = dt("s_ulp", [T, D], F16, kind=kd).ap()
        self.s_z2h = dt("s_z2h", [T, D], F16, kind=kd).ap()
        self.s_z2l = dt("s_z2l", [T, D], F16, kind=kd).ap()
        self.s_z3h = dt("s_z3h", [T, D], F16, kind=kd).ap()
        self.s_z3l = dt("s_z3l", [T, D], F16, kind=kd).ap()
        self.s_x1 = dt("s_x1", [T, D], F32, kind=kd).ap()
        self.s_x2 = dt("s_x2", [T, D], F32, kind=kd).ap()
        self.s_pb = dt("s_pb", [T, D], F32, kind=kd).ap()
        self.s_yp = dt("s_yp", [T, D], F32, kind=kd).ap()

        self.stack = ExitStack()
        st = self.stack
        self.a32 = Arena(st.enter_context(nc.sbuf_tensor("a32", [128, 19200], F32)), 19200)
        self.a16 = Arena(st.enter_context(nc.sbuf_tensor("a16", [128, 64512], F16)), 64512)
        self.ps = [st.enter_context(nc.psum_tensor(f"ps{i}", [128, 512], F32)).ap()
                   for i in range(8)]
        self.bar_sem = st.enter_context(nc.semaphore("bar"))
        self.bar_n = 0
        self.ld = Ctr(st, nc, "ld")
        self.tr = Ctr(st, nc, "tr")
        self.pf = Ctr(st, nc, "pf")
        self.wl = Ctr(st, nc, "wl")
        self.st = Ctr(st, nc, "st")
        self.ted = [Ctr(st, nc, f"ted{i}") for i in range(8)]
        self.psf = [Ctr(st, nc, f"psf{i}") for i in range(8)]
        self.vs = Ctr(st, nc, "vs")
        self.ac = Ctr(st, nc, "ac")
        self.gp = Ctr(st, nc, "gp")
        self.tf = Ctr(st, nc, "tf")
        self.uses = [0] * 8

        cdata = np.tile(np.array([EPS_LN, EPS_RMS, 1.0 / D, 0.0],
                                 dtype=np.float32), (128, 1))
        self.i_cst = nc.inline_tensor(cdata, name="consts").ap()
        self.cst = self.a32.alloc(4)
        self.stM = self.a32.alloc(32)
        self.stR = self.a32.alloc(32)
        self.stR2 = self.a32.alloc(32)
        self.lam = self.a32.alloc(2)
        self.carry = self.a32.alloc(2)
        self.dws = self.a32.alloc(D)
        self.wsm8 = self.a32.alloc(8)
        self.ffw8 = self.a32.alloc(8)
        self.tiny = self.a32.alloc(64)
        self.dump = self.a32.alloc(1024)

    def barrier(self):
        nc = self.nc
        self.bar_n += 5
        tgt = self.bar_n
        with nc.Block() as blk:
            for name in ("sync", "vector", "scalar", "tensor", "gpsimd"):
                def body(eng, t=tgt):
                    eng.drain(fusable=False).then_inc(self.bar_sem, 1)
                    eng.wait_ge(self.bar_sem, t)
                getattr(blk, name)(body)

    def bank_acquire(self, te, b, use):
        if use > 1:
            te.wait_ge(self.psf[b].sem, use - 1)

    def phase_a_init(self):
        nc = self.nc
        a32, a16 = self.a32, self.a16
        self.bw_h = a16.alloc(2048)
        self.bw_l = a16.alloc(2048)
        self.cw_h = a16.alloc(2048)
        self.cw_l = a16.alloc(2048)
        m32 = a32.mark()
        wst = a32.alloc(2048)
        lamraw = a32.alloc(2)
        dskb = a32.alloc(D)
        wsmb = a32.alloc(D)

        ld, vs, ac = self.ld, self.vs, self.ac
        t_ld1 = ld.inc(16 * 7)
        t_vs1 = vs.inc(1)
        t_ld2 = ld.inc(16)
        t_vs2 = vs.inc(1)
        t_ac1 = ac.inc(1)

        with nc.Block() as blk:
            @blk.sync
            def _(s):
                s.dma_start(out=lamraw[:, :],
                            in_=self.i_ll.rearrange("(c p) -> p c", p=128),
                            allow_slow_non_contiguous=True
                            ).then_inc(ld.sem, 16)
                s.dma_start(out=self.wsm8[:, :],
                            in_=self.i_wsm.rearrange("(k p) -> p k", p=128),
                            allow_slow_non_contiguous=True
                            ).then_inc(ld.sem, 16)
                s.dma_start(out=self.ffw8[:, :],
                            in_=self.i_wff.rearrange("(k p) -> p k", p=128),
                            allow_slow_non_contiguous=True
                            ).then_inc(ld.sem, 16)
                s.dma_start(out=dskb[:, :],
                            in_=self.i_dsk.unsqueeze(0).broadcast_to([128, D])
                            ).then_inc(ld.sem, 16)
                s.dma_start(out=wsmb[:, :],
                            in_=self.i_wsm.unsqueeze(0).broadcast_to([128, D])
                            ).then_inc(ld.sem, 16)
                s.dma_start(out=self.cst[:, :], in_=self.i_cst
                            ).then_inc(ld.sem, 16)
                s.dma_start(out=wst[:, :].rearrange("p (k n) -> p k n", k=KC),
                            in_=self.i_bwt.rearrange("(k p) n -> p k n", p=128)
                            ).then_inc(ld.sem, 16)
                s.wait_ge(vs.sem, t_vs1)
                s.dma_start(out=wst[:, :].rearrange("p (c d) -> p c d", c=2),
                            in_=self.i_cwt.rearrange("(c p) d -> p c d", p=128)
                            ).then_inc(ld.sem, 16)

            @blk.scalar
            def _(sc):
                sc.wait_ge(ld.sem, t_ld1)
                sc.activation(self.lam[:, :], lamraw[:, :], AF.Sigmoid
                              )
                sc.drain(fusable=False).then_inc(ac.sem, 1)

            @blk.vector
            def _(v):
                v.wait_ge(ld.sem, t_ld1)
                v.tensor_mul(self.dws[:, :], wsmb[:, :], dskb[:, :])
                for k in range(KC):
                    w = wst[:, k * 256:(k + 1) * 256]
                    hi = self.bw_h[:, k * 256:(k + 1) * 256]
                    lo = self.bw_l[:, k * 256:(k + 1) * 256]
                    v.tensor_scalar(hi, w, self.wsm8[:, k:k + 1], None,
                                    op0=ALU.mult)
                    v.drain(fusable=False)
                    i = v.scalar_tensor_tensor(lo, w, self.wsm8[:, k:k + 1], hi,
                                               op0=ALU.mult, op1=ALU.subtract)
                v.drain(fusable=False).then_inc(vs.sem, 1)
                v.wait_ge(ld.sem, t_ld2)
                v.tensor_copy(self.cw_h[:, :], wst[:, :])
                v.drain(fusable=False)
                v.tensor_sub(self.cw_l[:, :], wst[:, :], self.cw_h[:, :]
                             )
                v.drain(fusable=False).then_inc(vs.sem, 1)
        a32.reset(m32)

    def phase_a1(self, src):
        nc = self.nc
        a32, a16 = self.a32, self.a16
        m32, m16 = a32.mark(), a16.mark()
        xsl = [a32.alloc(4096), a32.alloc(4096)]
        z1b = a32.alloc(4096)
        usl = [(a16.alloc(4096), a16.alloc(4096)) for _ in range(2)]
        sS = self.tiny[:, 0:4]
        sQ = self.tiny[:, 4:8]
        sV = self.tiny[:, 8:12]
        sSD = self.tiny[:, 12:16]
        sQ2 = self.tiny[:, 16:20]
        sSD2 = self.tiny[:, 20:24]
        sVa = self.tiny[:, 40:44]

        x_src = self.i_x if src else self.i_xp
        uh_st = self.s_uh if src else self.s_uhp
        ul_st = self.s_ul if src else self.s_ulp

        ld, vs, ac, st = self.ld, self.vs, self.ac, self.st
        serial = getattr(self, "a1_serial", False)
        sch = []
        for ts in range(NTS):
            it = {"ts": ts, "slot": ts % 2,
                  "ld": ld.inc(16),
                  "ac1": ac.inc(1), "vs1": vs.inc(1),
                  "ac2": ac.inc(1), "vs2": vs.inc(1),
                  "ac3": ac.inc(1), "vs3": vs.inc(1),
                  "st": st.inc(32)}
            sch.append(it)

        with nc.Block() as blk:
            @blk.sync
            def _(s):
                for it in sch:
                    ts, sl = it["ts"], it["slot"]
                    if serial and ts >= 1:
                        s.wait_ge(st.sem, sch[ts - 1]["st"])
                    if ts >= 2:
                        s.wait_ge(vs.sem, sch[ts - 2]["vs2"])
                    s.dma_start(
                        out=xsl[sl][:, :].rearrange("p (a d) -> p a d", a=4),
                        in_=x_src[ts * TS:(ts + 1) * TS, :].rearrange(
                            "(a p) d -> p a d", p=128)).then_inc(ld.sem, 16)

            @blk.scalar
            def _(sc):
                for it in sch:
                    ts, sl = it["ts"], it["slot"]
                    if serial and ts >= 1:
                        sc.wait_ge(st.sem, sch[ts - 1]["st"])
                    sc.wait_ge(ld.sem, it["ld"])
                    for r in range(4):
                        xt = xsl[sl][:, r * 1024:(r + 1) * 1024]
                        sc.activation(self.dump[:, :], xt, AF.Identity,
                                      accum_out=sS[:, r:r + 1])
                    for r in range(4):
                        xt = xsl[sl][:, r * 1024:(r + 1) * 1024]
                        sc.activation(self.dump[:, :], xt, AF.Square,
                                      accum_out=sQ[:, r:r + 1])
                    sc.drain(fusable=False).then_inc(ac.sem, 1)
                    sc.wait_ge(vs.sem, it["vs1"])
                    sc.activation(sSD[:, :], sV[:, :], AF.Sqrt,
                                  bias=self.cst[:, 0:1])
                    sc.drain(fusable=False).then_inc(ac.sem, 1)
                    sc.wait_ge(vs.sem, it["vs2"])
                    for r in range(4):
                        z1 = z1b[:, r * 1024:(r + 1) * 1024]
                        sc.activation(self.dump[:, :], z1, AF.Square,
                                      accum_out=sQ2[:, r:r + 1])
                    sc.drain(fusable=False)
                    sc.activation(sSD2[:, :], sQ2[:, :], AF.Sqrt,
                                  bias=self.cst[:, 1:2],
                                  scale=self.cst[:, 2:3])
                    sc.drain(fusable=False).then_inc(ac.sem, 1)

            @blk.vector
            def _(v):
                for it in sch:
                    ts, sl = it["ts"], it["slot"]
                    c0 = ts * 4
                    if serial and ts >= 1:
                        v.wait_ge(st.sem, sch[ts - 1]["st"])
                    v.wait_ge(ac.sem, it["ac1"])
                    v.tensor_single_scalar(self.stM[:, c0:c0 + 4], sS[:, :],
                                           1.0 / D, op=ALU.mult)
                    v.drain(fusable=False)
                    v.tensor_mul(sVa[:, :], self.stM[:, c0:c0 + 4],
                                 self.stM[:, c0:c0 + 4])
                    v.drain(fusable=False)
                    v.scalar_tensor_tensor(sV[:, :], sQ[:, :], 1.0 / D,
                                           sVa[:, :], op0=ALU.mult,
                                           op1=ALU.subtract)
                    v.drain(fusable=False).then_inc(vs.sem, 1)
                    v.wait_ge(ac.sem, it["ac2"])
                    v.reciprocal(self.stR[:, c0:c0 + 4], sSD[:, :])
                    v.drain(fusable=False)
                    last = None
                    for r in range(4):
                        xt = xsl[sl][:, r * 1024:(r + 1) * 1024]
                        z1 = z1b[:, r * 1024:(r + 1) * 1024]
                        last = v.tensor_scalar(z1, xt,
                                               self.stM[:, c0 + r:c0 + r + 1],
                                               self.stR[:, c0 + r:c0 + r + 1],
                                               op0=ALU.subtract, op1=ALU.mult)
                    v.drain(fusable=False).then_inc(vs.sem, 1)
                    v.wait_ge(ac.sem, it["ac3"])
                    v.reciprocal(self.stR2[:, c0:c0 + 4], sSD2[:, :])
                    v.drain(fusable=False)
                    if ts >= 2:
                        v.wait_ge(st.sem, sch[ts - 2]["st"])
                    uh, ul = usl[sl]
                    last = None
                    for r in range(4):
                        z1 = z1b[:, r * 1024:(r + 1) * 1024]
                        r2 = self.stR2[:, c0 + r:c0 + r + 1]
                        v.tensor_scalar(uh[:, r * 1024:(r + 1) * 1024], z1, r2,
                                        None, op0=ALU.mult)
                        last = v.scalar_tensor_tensor(
                            ul[:, r * 1024:(r + 1) * 1024], z1, r2,
                            uh[:, r * 1024:(r + 1) * 1024],
                            op0=ALU.mult, op1=ALU.subtract)
                    v.drain(fusable=False).then_inc(vs.sem, 1)

            @blk.gpsimd
            def _(g):
                for it in sch:
                    ts, sl = it["ts"], it["slot"]
                    g.wait_ge(vs.sem, it["vs3"])
                    uh, ul = usl[sl]
                    g.dma_start(
                        out=uh_st[ts * TS:(ts + 1) * TS, :].rearrange(
                            "(a p) d -> p a d", p=128),
                        in_=uh[:, :].rearrange("p (a d) -> p a d", a=4)
                        ).then_inc(st.sem, 16)
                    g.dma_start(
                        out=ul_st[ts * TS:(ts + 1) * TS, :].rearrange(
                            "(a p) d -> p a d", p=128),
                        in_=ul[:, :].rearrange("p (a d) -> p a d", a=4)
                        ).then_inc(st.sem, 16)

        a32.reset(m32)
        a16.reset(m16)

    def phase_a2(self, src):
        nc = self.nc
        a32, a16 = self.a32, self.a16
        m32, m16 = a32.mark(), a16.mark()
        ufh = [a16.alloc(4096), a16.alloc(4096)]
        ufl = [a16.alloc(4096), a16.alloc(4096)]
        bu32 = a32.alloc(2048)
        hs32 = a32.alloc(2048)

        uh_st = self.s_uh if src else self.s_uhp
        ul_st = self.s_ul if src else self.s_ulp
        tr, vs, tf = self.tr, self.vs, self.tf
        st0 = self.st.n
        BU = [6, 7]

        sch = []
        for ts in range(NTS):
            it = {"ts": ts, "slot": ts % 2, "tr": tr.inc(16 * 16), "use": []}
            for nc2 in range(2):
                b = BU[nc2]
                self.uses[b] += 1
                it["use"].append(self.uses[b])
            it["vs"] = vs.inc(1)
            it["tf"] = tf.inc(1)
            sch.append(it)

        with nc.Block() as blk:
            @blk.sync
            def _(s):
                if st0:
                    s.wait_ge(self.st.sem, st0)
                for it in sch:
                    ts, sl = it["ts"], it["slot"]
                    if ts >= 2:
                        s.wait_ge(tf.sem, sch[ts - 2]["tf"])
                    for k in range(KC):
                        s.dma_start(out=ufh[sl][:, k * 512:(k + 1) * 512],
                                    in_=uh_st[ts * TS:(ts + 1) * TS,
                                              k * 128:(k + 1) * 128],
                                    transpose=True).then_inc(tr.sem, 16)
                        s.dma_start(out=ufl[sl][:, k * 512:(k + 1) * 512],
                                    in_=ul_st[ts * TS:(ts + 1) * TS,
                                              k * 128:(k + 1) * 128],
                                    transpose=True).then_inc(tr.sem, 16)

            @blk.tensor
            def _(te):
                for it in sch:
                    ts, sl = it["ts"], it["slot"]
                    te.wait_ge(tr.sem, it["tr"])
                    for nc2 in range(2):
                        b = BU[nc2]
                        use = it["use"][nc2]
                        self.bank_acquire(te, b, use)
                        last = None
                        for k in range(KC):
                            bh = self.bw_h[:, k * 256 + nc2 * 128:
                                           k * 256 + nc2 * 128 + 128]
                            bl = self.bw_l[:, k * 256 + nc2 * 128:
                                           k * 256 + nc2 * 128 + 128]
                            xh = ufh[sl][:, k * 512:(k + 1) * 512]
                            xl = ufl[sl][:, k * 512:(k + 1) * 512]
                            te.ldweights(bh)
                            te.matmul(self.ps[b][:, :], bh, xh,
                                      start=(k == 0), stop=False)
                            te.matmul(self.ps[b][:, :], bh, xl,
                                      start=False, stop=False)
                            te.ldweights(bl)
                            last = te.matmul(self.ps[b][:, :], bl, xh,
                                             start=False, stop=(k == KC - 1))
                        last.then_inc(self.ted[b].sem, 1)
                    te.nop().then_inc(tf.sem, 1)

            @blk.vector
            def _(v):
                for it in sch:
                    ts, sl = it["ts"], it["slot"]
                    for nc2 in range(2):
                        b = BU[nc2]
                        use = it["use"][nc2]
                        v.wait_ge(self.ted[b].sem, use)
                        bu = bu32[:, (sl * 2 + nc2) * 512:(sl * 2 + nc2 + 1) * 512]
                        hs = hs32[:, (sl * 2 + nc2) * 512:(sl * 2 + nc2 + 1) * 512]
                        v.tensor_copy(bu, self.ps[b][:, :]
                                      ).then_inc(self.psf[b].sem, 1)
                        if ts == 0:
                            init = self.carry[:, nc2:nc2 + 1] if src else 0.0
                        else:
                            psl = 1 - sl
                            init = hs32[:, (psl * 2 + nc2) * 512 + 511:
                                        (psl * 2 + nc2) * 512 + 512]
                        v.tensor_tensor_scan(
                            hs, self.lam[:, nc2:nc2 + 1].broadcast_to([128, 512]),
                            bu, init, op0=ALU.mult, op1=ALU.add)
                        if src:
                            hh = self.hsp_h[:, nc2 * T + ts * 512:
                                            nc2 * T + (ts + 1) * 512]
                            hl = self.hsp_l[:, nc2 * T + ts * 512:
                                            nc2 * T + (ts + 1) * 512]
                            v.tensor_copy(hh, hs)
                            v.drain(fusable=False)
                            v.scalar_tensor_tensor(hl, hs, 1.0, hh,
                                                   op0=ALU.mult,
                                                   op1=ALU.subtract)
                        elif ts == NTS - 1:
                            v.drain(fusable=False)
                            v.tensor_copy(self.carry[:, nc2:nc2 + 1],
                                          hs[:, 511:512])
                    v.nop()
                    v.drain(fusable=False).then_inc(vs.sem, 1)

        a32.reset(m32)
        a16.reset(m16)

    def phase_a3(self):
        nc = self.nc
        a32, a16 = self.a32, self.a16
        m32, m16 = a32.mark(), a16.mark()
        xre = [a32.alloc(1024), a32.alloc(1024)]
        z1r = [a32.alloc(1024), a32.alloc(1024)]
        dub = [a32.alloc(1024), a32.alloc(1024)]
        x1a = [a32.alloc(1024), a32.alloc(1024)]
        x1g = [a32.alloc(1024), a32.alloc(1024)]
        z2s = [(a16.alloc(1024), a16.alloc(1024)) for _ in range(2)]
        sQ3 = self.tiny[:, 24:25]
        sSD3 = self.tiny[:, 25:26]
        sR3 = self.tiny[:, 26:27]

        ld, vs, ac, st, gp = self.ld, self.vs, self.ac, self.st, self.gp
        sch = []
        for tch in range(T // 128):
            it = {"tch": tch, "slot": tch % 2, "ld": ld.inc(16), "use": []}
            for dh in range(2):
                b = (tch % 2) * 2 + dh
                self.uses[b] += 1
                it["use"].append(self.uses[b])
            it["vs1"] = vs.inc(1)
            it["gp1"] = gp.inc(1)
            it["ac1"] = ac.inc(1)
            it["vs2"] = vs.inc(1)
            it["st"] = st.inc(48)
            sch.append(it)

        with nc.Block() as blk:
            @blk.sync
            def _(s):
                for it in sch:
                    tch, sl = it["tch"], it["slot"]
                    if tch >= 2:
                        s.wait_ge(vs.sem, sch[tch - 2]["vs1"])
                    s.dma_start(out=xre[sl][:, :],
                                in_=self.i_x[tch * 128:(tch + 1) * 128, :]
                                ).then_inc(ld.sem, 16)

            @blk.tensor
            def _(te):
                for it in sch:
                    tch = it["tch"]
                    t0 = tch * 128
                    for dh in range(2):
                        b = (tch % 2) * 2 + dh
                        use = it["use"][dh]
                        self.bank_acquire(te, b, use)
                        last = None
                        for nc2 in range(2):
                            hh = self.hsp_h[:, nc2 * T + t0:nc2 * T + t0 + 128]
                            hl = self.hsp_l[:, nc2 * T + t0:nc2 * T + t0 + 128]
                            ch = self.cw_h[:, nc2 * 1024 + dh * 512:
                                           nc2 * 1024 + dh * 512 + 512]
                            cl = self.cw_l[:, nc2 * 1024 + dh * 512:
                                           nc2 * 1024 + dh * 512 + 512]
                            te.ldweights(hh)
                            te.matmul(self.ps[b][:, :], hh, ch,
                                      start=(nc2 == 0), stop=False)
                            te.matmul(self.ps[b][:, :], hh, cl,
                                      start=False, stop=False)
                            te.ldweights(hl)
                            last = te.matmul(self.ps[b][:, :], hl, ch,
                                             start=False, stop=(nc2 == 1))
                        last.then_inc(self.ted[b].sem, 1)

            @blk.vector
            def _(v):
                for it in sch:
                    tch, sl = it["tch"], it["slot"]
                    c = tch
                    if tch >= 2:
                        v.wait_ge(st.sem, sch[tch - 2]["st"])
                    v.wait_ge(ld.sem, it["ld"])
                    v.tensor_scalar(z1r[sl][:, :], xre[sl][:, :],
                                    self.stM[:, c:c + 1], self.stR[:, c:c + 1],
                                    op0=ALU.subtract, op1=ALU.mult)
                    v.scalar_tensor_tensor(dub[sl][:, :], z1r[sl][:, :],
                                           self.stR2[:, c:c + 1], self.dws[:, :],
                                           op0=ALU.mult, op1=ALU.mult)
                    last = None
                    for dh in range(2):
                        b = (tch % 2) * 2 + dh
                        v.wait_ge(self.ted[b].sem, it["use"][dh])
                        last = v.tensor_tensor(
                            x1a[sl][:, dh * 512:(dh + 1) * 512],
                            self.ps[b][:, :],
                            z1r[sl][:, dh * 512:(dh + 1) * 512], op=ALU.add)
                        last.then_inc(self.psf[b].sem, 1)
                    v.nop()
                    v.drain(fusable=False).then_inc(vs.sem, 1)
                    v.wait_ge(ac.sem, it["ac1"])
                    v.reciprocal(sR3[:, :], sSD3[:, :])
                    v.drain(fusable=False)
                    zh, zl = z2s[sl]
                    v.tensor_scalar(zh[:, :], x1g[sl][:, :], sR3[:, :], None,
                                    op0=ALU.mult)
                    v.scalar_tensor_tensor(zl[:, :], x1g[sl][:, :], sR3[:, :],
                                           zh[:, :], op0=ALU.mult,
                                           op1=ALU.subtract)
                    v.drain(fusable=False).then_inc(vs.sem, 1)

            @blk.gpsimd
            def _(g):
                for it in sch:
                    tch, sl = it["tch"], it["slot"]
                    g.wait_ge(vs.sem, it["vs1"])
                    g.tensor_add(x1g[sl][:, :], x1a[sl][:, :], dub[sl][:, :]
                                 )
                    g.drain(fusable=False).then_inc(gp.sem, 1)
                    g.wait_ge(vs.sem, it["vs2"])
                    zh, zl = z2s[sl]
                    rows = slice(tch * 128, (tch + 1) * 128)
                    g.dma_start(out=self.s_x1[rows, :],
                                in_=x1g[sl][:, :]).then_inc(st.sem, 16)
                    g.dma_start(out=self.s_z2h[rows, :],
                                in_=zh[:, :]).then_inc(st.sem, 16)
                    g.dma_start(out=self.s_z2l[rows, :],
                                in_=zl[:, :]).then_inc(st.sem, 16)

            @blk.scalar
            def _(sc):
                for it in sch:
                    tch, sl = it["tch"], it["slot"]
                    sc.wait_ge(gp.sem, it["gp1"])
                    sc.activation(self.dump[:, :], x1g[sl][:, :], AF.Square,
                                  accum_out=sQ3[:, :])
                    sc.activation(sSD3[:, :], sQ3[:, :], AF.Sqrt,
                                  bias=self.cst[:, 1:2],
                                  scale=self.cst[:, 2:3])
                    sc.drain(fusable=False).then_inc(ac.sem, 1)

        a32.reset(m32)
        a16.reset(m16)

    def phase_b(self, q):
        nc = self.nc
        tiles = BQ[q]
        nj = len(tiles)
        widths = [FT_W[j] for j in tiles]
        offs = [FT_OFF[j] - FT_OFF[tiles[0]] for j in tiles]
        fq = sum(widths)
        c0w = FT_OFF[tiles[0]]
        a32, a16 = self.a32, self.a16
        m32, m16 = a32.mark(), a16.mark()
        w1h = a16.alloc(KC * fq)
        w1l = a16.alloc(KC * fq)
        w3h = a16.alloc(KC * fq)
        w3l = a16.alloc(KC * fq)
        w2h = a16.alloc(nj * 1024)
        w2l = a16.alloc(nj * 1024)
        zfh = [a16.alloc(KC * 512) for _ in range(2)]
        zfl = [a16.alloc(KC * 512) for _ in range(2)]
        gvh = a16.alloc(nj * 512)
        gvl = a16.alloc(nj * 512)
        if q == 3:
            z3h = [a16.alloc(1024), a16.alloc(1024)]
            z3l = [a16.alloc(1024), a16.alloc(1024)]
            sS4 = self.tiny[:, 32:33]
            sQ4 = self.tiny[:, 33:34]
            sV4 = self.tiny[:, 34:35]
            sSD4 = self.tiny[:, 35:36]
            sV4a = self.tiny[:, 38:39]
            sM4 = self.tiny[:, 36:37]
            sR4 = self.tiny[:, 37:38]
        mw32 = a32.mark()
        wst = [a32.alloc(1024), a32.alloc(1024)]

        wl, vs, ac, st, tr, pf, gp, tf = (self.wl, self.vs, self.ac, self.st,
                                          self.tr, self.pf, self.gp, self.tf)

        wsch = [{"wl": wl.inc(16), "vs": vs.inc(1)} for _ in range(2 * KC + nj)]
        with nc.Block() as blk:
            @blk.sync
            def _(s):
                idx = 0
                for hbm in (self.i_w1t, self.i_w3t):
                    for k in range(KC):
                        if idx >= 2:
                            s.wait_ge(vs.sem, wsch[idx - 2]["vs"])
                        s.dma_start(out=wst[idx % 2][:, 0:fq],
                                    in_=hbm[k * 128:(k + 1) * 128,
                                            c0w:c0w + fq]).then_inc(wl.sem, 16)
                        idx += 1
                for ji in range(nj):
                    if idx >= 2:
                        s.wait_ge(vs.sem, wsch[idx - 2]["vs"])
                    s.dma_start(
                        out=wst[idx % 2][0:widths[ji], :],
                        in_=self.i_w2t[c0w + offs[ji]:c0w + offs[ji] + widths[ji],
                                       :]).then_inc(wl.sem, 16)
                    idx += 1

            @blk.vector
            def _(v):
                idx = 0
                for dsth, dstl in ((w1h, w1l), (w3h, w3l)):
                    for k in range(KC):
                        v.wait_ge(wl.sem, wsch[idx]["wl"])
                        w = wst[idx % 2][:, 0:fq]
                        hi = dsth[:, k * fq:(k + 1) * fq]
                        lo = dstl[:, k * fq:(k + 1) * fq]
                        v.tensor_scalar(hi, w, self.ffw8[:, k:k + 1], None,
                                        op0=ALU.mult)
                        v.drain(fusable=False)
                        v.scalar_tensor_tensor(lo, w, self.ffw8[:, k:k + 1],
                                               hi, op0=ALU.mult,
                                               op1=ALU.subtract
                                               )
                        v.drain(fusable=False).then_inc(vs.sem, 1)
                        idx += 1
                for ji in range(nj):
                    v.wait_ge(wl.sem, wsch[idx]["wl"])
                    w = wst[idx % 2][0:widths[ji], :]
                    hi = w2h[0:widths[ji], ji * 1024:(ji + 1) * 1024]
                    lo = w2l[0:widths[ji], ji * 1024:(ji + 1) * 1024]
                    v.tensor_copy(hi, w)
                    v.drain(fusable=False)
                    v.tensor_sub(lo, w, hi)
                    v.drain(fusable=False).then_inc(vs.sem, 1)
                    idx += 1
        vs_w_end = vs.n
        a32.reset(mw32)
        NSL = 4
        sg = [a32.alloc(512), a32.alloc(512)]
        gv32 = [a32.alloc(512), a32.alloc(512)]
        pin = [a32.alloc(512) for _ in range(NSL)]
        pout = [a32.alloc(512) for _ in range(NSL if q < 3 else 2)]
        if q == 3:
            x1r = [a32.alloc(512) for _ in range(NSL)]
            xr = [a32.alloc(512) for _ in range(NSL)]
            tbuf = [a32.alloc(512), a32.alloc(512)]
            x2b = [a32.alloc(1024) for _ in range(4)]
            z3f = [a32.alloc(1024), a32.alloc(1024)]

        st0 = self.st.n
        sch = []
        wb_flat = []
        z3_flat = []
        jj_flat = []
        for tg in range(NTGB):
            it = {"tg": tg, "slot": tg % 2, "tr": tr.inc(16 * 16), "jj": []}
            for ji in range(nj):
                grp = ji % 2
                bg, bv = 2 * grp, 2 * grp + 1
                self.uses[bg] += 1
                self.uses[bv] += 1
                jit = {"ji": ji, "bg": bg, "bv": bv,
                       "ug": self.uses[bg], "uv": self.uses[bv],
                       "ac": ac.inc(1), "vs": vs.inc(1)}
                it["jj"].append(jit)
                jj_flat.append(jit)
            it["gv_done"] = vs.n
            it["wb"] = []
            for tc in range(TGB // 128):
                for dh in range(2):
                    b = 4 + (tc * 2 + dh) % 4
                    self.uses[b] += 1
                    wit = {"tg": tg, "tc": tc, "dh": dh, "b": b,
                           "u": (tc * 2 + dh) % 4, "use": self.uses[b],
                           "pf": None, "st": None}
                    if q > 0:
                        wit["pf"] = pf.inc(16)
                    if q == 3:
                        wit["pf3"] = pf.inc(32)
                    wit["vs"] = vs.inc(1)
                    if q < 3:
                        wit["st"] = st.inc(16)
                    it["wb"].append(wit)
                    wb_flat.append(wit)
            it["tf"] = tf.inc(1)
            if q == 3:
                it["z3"] = []
                for tc in range(TGB // 128):
                    zit = {"tg": tg, "tc": tc,
                           "ac1": ac.inc(1), "vs0": vs.inc(1),
                           "ac2": ac.inc(1), "vs": vs.inc(1),
                           "st": st.inc(48)}
                    it["z3"].append(zit)
                    z3_flat.append(zit)
            sch.append(it)

        with nc.Block() as blk:
            @blk.sync
            def _(s):
                if st0:
                    s.wait_ge(self.st.sem, st0)
                for it in sch:
                    tg, sl = it["tg"], it["slot"]
                    t0 = tg * TGB
                    if tg >= 2:
                        s.wait_ge(tf.sem, sch[tg - 2]["tf"])
                    for k in range(KC):
                        s.dma_start(out=zfh[sl][:, k * 512:(k + 1) * 512],
                                    in_=self.s_z2h[t0:t0 + TGB,
                                                   k * 128:(k + 1) * 128],
                                    transpose=True).then_inc(tr.sem, 16)
                        s.dma_start(out=zfl[sl][:, k * 512:(k + 1) * 512],
                                    in_=self.s_z2l[t0:t0 + TGB,
                                                   k * 128:(k + 1) * 128],
                                    transpose=True).then_inc(tr.sem, 16)
                    for wit in it["wb"]:
                        tc, dh, u = wit["tc"], wit["dh"], wit["u"]
                        gi = wb_flat.index(wit)
                        rows = slice(t0 + tc * 128, t0 + (tc + 1) * 128)
                        cols = slice(dh * 512, (dh + 1) * 512)
                        if gi >= 4 and (q > 0 or q == 3):
                            s.wait_ge(vs.sem, wb_flat[gi - 4]["vs"])
                        if q > 0:
                            s.dma_start(out=pin[u][:, :],
                                        in_=self.s_pb[rows, cols]
                                        ).then_inc(pf.sem, 16)
                        if q == 3:
                            s.dma_start(out=x1r[u][:, :],
                                        in_=self.s_x1[rows, cols]
                                        ).then_inc(pf.sem, 16)
                            s.dma_start(out=xr[u][:, :],
                                        in_=self.i_x[rows, cols]
                                        ).then_inc(pf.sem, 16)

            @blk.tensor
            def _(te):
                te.wait_ge(vs.sem, vs_w_end)
                for it in sch:
                    tg, sl = it["tg"], it["slot"]
                    te.wait_ge(tr.sem, it["tr"])
                    for jit in it["jj"]:
                        ji = jit["ji"]
                        wj = widths[ji]
                        for b, wh_, wl_, use in ((jit["bg"], w1h, w1l, jit["ug"]),
                                                 (jit["bv"], w3h, w3l, jit["uv"])):
                            self.bank_acquire(te, b, use)
                            last = None
                            for k in range(KC):
                                co = k * fq + offs[ji]
                                wh = wh_[:, co:co + wj]
                                wlo = wl_[:, co:co + wj]
                                xh = zfh[sl][:, k * 512:(k + 1) * 512]
                                xl = zfl[sl][:, k * 512:(k + 1) * 512]
                                te.ldweights(wh)
                                te.matmul(self.ps[b][0:wj, :], wh, xh,
                                          start=(k == 0), stop=False)
                                te.matmul(self.ps[b][0:wj, :], wh, xl,
                                          start=False, stop=False)
                                te.ldweights(wlo)
                                last = te.matmul(self.ps[b][0:wj, :], wlo, xh,
                                                 start=False,
                                                 stop=(k == KC - 1))
                            last.then_inc(self.ted[b].sem, 1)
                    te.wait_ge(vs.sem, it["gv_done"])
                    for wit in it["wb"]:
                        tc, dh, b, use = wit["tc"], wit["dh"], wit["b"], wit["use"]
                        self.bank_acquire(te, b, use)
                        last = None
                        for ji in range(nj):
                            wj = widths[ji]
                            gh = gvh[0:wj, ji * 512 + tc * 128:
                                     ji * 512 + tc * 128 + 128]
                            gl = gvl[0:wj, ji * 512 + tc * 128:
                                     ji * 512 + tc * 128 + 128]
                            rh = w2h[0:wj, ji * 1024 + dh * 512:
                                     ji * 1024 + dh * 512 + 512]
                            rl = w2l[0:wj, ji * 1024 + dh * 512:
                                     ji * 1024 + dh * 512 + 512]
                            te.ldweights(gh)
                            te.matmul(self.ps[b][:, :], gh, rh,
                                      start=(ji == 0), stop=False)
                            te.matmul(self.ps[b][:, :], gh, rl,
                                      start=False, stop=False)
                            te.ldweights(gl)
                            last = te.matmul(self.ps[b][:, :], gl, rh,
                                             start=False, stop=(ji == nj - 1))
                        last.then_inc(self.ted[b].sem, 1)
                    te.nop().then_inc(tf.sem, 1)

            @blk.scalar
            def _(sc):
                for it in sch:
                    for jit in it["jj"]:
                        ji, bg = jit["ji"], jit["bg"]
                        wj = widths[ji]
                        u = ji % 2
                        gj = jj_flat.index(jit)
                        if gj >= 2:
                            sc.wait_ge(vs.sem, jj_flat[gj - 2]["vs"])
                        sc.wait_ge(self.ted[bg].sem, jit["ug"])
                        sc.activation(sg[u][0:wj, :], self.ps[bg][0:wj, :],
                                      AF.Silu)
                        sc.drain(fusable=False).then_inc(ac.sem, 1)
                    if q == 3:
                        for zit in it["z3"]:
                            tc = zit["tc"]
                            gi = z3_flat.index(zit)
                            if gi >= 1:
                                sc.wait_ge(vs.sem, z3_flat[gi - 1]["vs0"])
                            sc.wait_ge(vs.sem, it["wb"][tc * 2 + 1]["vs"])
                            sc.activation(self.dump[:, :], x2b[tc][:, :],
                                          AF.Identity, accum_out=sS4[:, :])
                            sc.activation(self.dump[:, :], x2b[tc][:, :],
                                          AF.Square, accum_out=sQ4[:, :])
                            sc.drain(fusable=False).then_inc(ac.sem, 1)
                            sc.wait_ge(vs.sem, zit["vs0"])
                            sc.activation(sSD4[:, :], sV4[:, :], AF.Sqrt,
                                          bias=self.cst[:, 0:1])
                            sc.drain(fusable=False).then_inc(ac.sem, 1)

            @blk.vector
            def _(v):
                for it in sch:
                    tg, sl = it["tg"], it["slot"]
                    if tg >= 1:
                        v.wait_ge(tf.sem, sch[tg - 1]["tf"])
                    for jit in it["jj"]:
                        ji, bg, bv_ = jit["ji"], jit["bg"], jit["bv"]
                        wj = widths[ji]
                        u = ji % 2
                        v.wait_ge(ac.sem, jit["ac"])
                        v.wait_ge(self.ted[bv_].sem, jit["uv"])
                        gh = gvh[0:wj, ji * 512:(ji + 1) * 512]
                        gl = gvl[0:wj, ji * 512:(ji + 1) * 512]
                        v.tensor_tensor(gh, sg[u][0:wj, :],
                                        self.ps[bv_][0:wj, :], op=ALU.mult
                                        ).then_inc(self.psf[bg].sem, 1)
                        v.tensor_tensor(gv32[u][0:wj, :], sg[u][0:wj, :],
                                        self.ps[bv_][0:wj, :], op=ALU.mult
                                        ).then_inc(self.psf[bv_].sem, 1)
                        v.tensor_sub(gl, gv32[u][0:wj, :], gh
                                     )
                        v.drain(fusable=False).then_inc(vs.sem, 1)
                    for wit in it["wb"]:
                        tc, dh, b, use = wit["tc"], wit["dh"], wit["b"], wit["use"]
                        u = wit["u"]
                        gi = wb_flat.index(wit)
                        v.wait_ge(self.ted[b].sem, use)
                        if q > 0:
                            v.wait_ge(pf.sem, wit["pf"])
                        if q == 3:
                            v.wait_ge(pf.sem, wit["pf3"])
                        if q < 3 and gi >= 4:
                            v.wait_ge(st.sem, wb_flat[gi - 4]["st"])
                        if q == 0:
                            v.tensor_copy(pout[u][:, :], self.ps[b][:, :]
                                          ).then_inc(self.psf[b].sem, 1)
                            v.nop()
                            v.drain(fusable=False).then_inc(vs.sem, 1)
                        elif q < 3:
                            i = v.tensor_tensor(pout[u][:, :], self.ps[b][:, :],
                                                pin[u][:, :], op=ALU.add)
                            i.then_inc(self.psf[b].sem, 1)
                            v.nop()
                            v.drain(fusable=False).then_inc(vs.sem, 1)
                        else:
                            idx3 = tg * 4 + tc
                            if idx3 >= 4:
                                v.wait_ge(st.sem, z3_flat[idx3 - 4]["st"])
                            v.tensor_tensor(pout[u % 2][:, :], self.ps[b][:, :],
                                            pin[u][:, :], op=ALU.add
                                            ).then_inc(self.psf[b].sem, 1)
                            v.tensor_add(tbuf[u % 2][:, :], pout[u % 2][:, :],
                                         x1r[u][:, :])
                            x2half = x2b[tc][:, dh * 512:(dh + 1) * 512]
                            v.tensor_add(x2half, tbuf[u % 2][:, :], xr[u][:, :]
                                         )
                            v.drain(fusable=False).then_inc(vs.sem, 1)
                    if q == 3:
                        for zit in it["z3"]:
                            tc = zit["tc"]
                            u = tc % 2
                            idx3 = tg * 4 + tc
                            if idx3 >= 2:
                                v.wait_ge(st.sem, z3_flat[idx3 - 2]["st"])
                            v.wait_ge(ac.sem, zit["ac1"])
                            v.tensor_single_scalar(sM4[:, :], sS4[:, :],
                                                   1.0 / D, op=ALU.mult)
                            v.drain(fusable=False)
                            v.tensor_mul(sV4a[:, :], sM4[:, :], sM4[:, :])
                            v.drain(fusable=False)
                            v.scalar_tensor_tensor(sV4[:, :], sQ4[:, :],
                                                   1.0 / D, sV4a[:, :],
                                                   op0=ALU.mult,
                                                   op1=ALU.subtract
                                                   )
                            v.drain(fusable=False).then_inc(vs.sem, 1)
                            v.wait_ge(ac.sem, zit["ac2"])
                            v.reciprocal(sR4[:, :], sSD4[:, :])
                            v.drain(fusable=False)
                            v.tensor_scalar(z3f[u][:, :], x2b[tc][:, :],
                                            sM4[:, :], sR4[:, :],
                                            op0=ALU.subtract, op1=ALU.mult)
                            v.tensor_copy(z3h[u][:, :], z3f[u][:, :])
                            v.tensor_sub(z3l[u][:, :], z3f[u][:, :],
                                         z3h[u][:, :])
                            v.drain(fusable=False).then_inc(vs.sem, 1)

            @blk.gpsimd
            def _(g):
                for it in sch:
                    tg = it["tg"]
                    t0 = tg * TGB
                    if q < 3:
                        for wit in it["wb"]:
                            tc, dh, u = wit["tc"], wit["dh"], wit["u"]
                            g.wait_ge(vs.sem, wit["vs"])
                            rows = slice(t0 + tc * 128, t0 + (tc + 1) * 128)
                            cols = slice(dh * 512, (dh + 1) * 512)
                            g.dma_start(out=self.s_pb[rows, cols],
                                        in_=pout[u][:, :]).then_inc(st.sem, 16)
                    else:
                        for zit in it["z3"]:
                            tc = zit["tc"]
                            u = tc % 2
                            g.wait_ge(vs.sem, zit["vs"])
                            rows = slice(t0 + tc * 128, t0 + (tc + 1) * 128)
                            g.dma_start(out=self.s_x2[rows, :],
                                        in_=x2b[tc][:, :]).then_inc(st.sem, 16)
                            g.dma_start(out=self.s_z3h[rows, :],
                                        in_=z3h[u][:, :]).then_inc(st.sem, 16)
                            g.dma_start(out=self.s_z3l[rows, :],
                                        in_=z3l[u][:, :]).then_inc(st.sem, 16)

        a32.reset(m32)
        a16.reset(m16)

    def phase_c(self, q):
        nc = self.nc
        a32, a16 = self.a32, self.a16
        m32, m16 = a32.mark(), a16.mark()
        fch = a16.alloc(KC * 1024)
        fcl = a16.alloc(KC * 1024)
        pjh = a16.alloc(8 * 1024)
        pjl = a16.alloc(8 * 1024)
        zfh = [a16.alloc(KC * 512) for _ in range(2)]
        zfl = [a16.alloc(KC * 512) for _ in range(2)]
        hp_h = a16.alloc(8 * 512)
        hp_l = a16.alloc(8 * 512)
        mw32 = a32.mark()
        wst = [a32.alloc(1024), a32.alloc(1024)]

        wl, vs, ac, st, tr, pf, gp, tf = (self.wl, self.vs, self.ac, self.st,
                                          self.tr, self.pf, self.gp, self.tf)
        f0 = CQ[q][0] * 128

        wsch = [{"wl": wl.inc(16), "vs": vs.inc(1)} for _ in range(KC + 8)]
        with nc.Block() as blk:
            @blk.sync
            def _(s):
                idx = 0
                for k in range(KC):
                    if idx >= 2:
                        s.wait_ge(vs.sem, wsch[idx - 2]["vs"])
                    s.dma_start(out=wst[idx % 2][:, :],
                                in_=self.i_fct[k * 128:(k + 1) * 128,
                                               f0:f0 + 1024]
                                ).then_inc(wl.sem, 16)
                    idx += 1
                for j in range(8):
                    if idx >= 2:
                        s.wait_ge(vs.sem, wsch[idx - 2]["vs"])
                    s.dma_start(out=wst[idx % 2][:, :],
                                in_=self.i_pjt[f0 + j * 128:f0 + (j + 1) * 128,
                                               :]).then_inc(wl.sem, 16)
                    idx += 1

            @blk.vector
            def _(v):
                idx = 0
                for k in range(KC):
                    v.wait_ge(wl.sem, wsch[idx]["wl"])
                    w = wst[idx % 2][:, :]
                    hi = fch[:, k * 1024:(k + 1) * 1024]
                    lo = fcl[:, k * 1024:(k + 1) * 1024]
                    v.tensor_copy(hi, w)
                    v.drain(fusable=False)
                    v.tensor_sub(lo, w, hi)
                    v.drain(fusable=False).then_inc(vs.sem, 1)
                    idx += 1
                for j in range(8):
                    v.wait_ge(wl.sem, wsch[idx]["wl"])
                    w = wst[idx % 2][:, :]
                    hi = pjh[:, j * 1024:(j + 1) * 1024]
                    lo = pjl[:, j * 1024:(j + 1) * 1024]
                    v.tensor_copy(hi, w)
                    v.drain(fusable=False)
                    v.tensor_sub(lo, w, hi)
                    v.drain(fusable=False).then_inc(vs.sem, 1)
                    idx += 1
        vs_w_end = vs.n
        a32.reset(mw32)
        NSL = 4
        r32 = [a32.alloc(512), a32.alloc(512)]
        h32 = [a32.alloc(512), a32.alloc(512)]
        pin = [a32.alloc(512) for _ in range(NSL)]
        pout = [a32.alloc(512) for _ in range(NSL)] if q < 3 else None
        if q == 3:
            x2r = [a32.alloc(512) for _ in range(NSL)]
            y2 = [a32.alloc(512) for _ in range(NSL)]
            inds = [a32.alloc(512) for _ in range(5)]
            gA = a32.alloc(512)
            gB = a32.alloc(512)
            t1b = a32.alloc(512)
            t2b = a32.alloc(512)
            ob = [a32.alloc(512) for _ in range(NSL)]

        st0 = self.st.n
        sch = []
        wb_flat = []
        jj_flat = []
        for tg in range(NTGC):
            it = {"tg": tg, "slot": tg % 2, "tr": tr.inc(16 * 16), "jj": []}
            for j in range(8):
                b = j % 4
                self.uses[b] += 1
                jit = {"j": j, "b": b, "use": self.uses[b],
                       "ac": ac.inc(1), "vs": vs.inc(1), "gp": gp.inc(1)}
                it["jj"].append(jit)
                jj_flat.append(jit)
            it["h_done_vs"] = vs.n
            it["h_done_gp"] = gp.n
            it["wb"] = []
            for tc in range(TGC // 128):
                for dh in range(2):
                    b = 4 + (tc * 2 + dh) % 4
                    self.uses[b] += 1
                    wit = {"tg": tg, "tc": tc, "dh": dh, "b": b,
                           "u": (tc * 2 + dh) % 4, "use": self.uses[b]}
                    if q > 0:
                        wit["pf"] = pf.inc(16)
                    if q == 3:
                        wit["pfx"] = pf.inc(16)
                    wit["vs"] = vs.inc(1)
                    if q == 3:
                        wit["vsi"] = vs.inc(1)
                        wit["gp"] = gp.inc(1)
                        wit["vsm"] = vs.inc(1)
                    wit["st"] = st.inc(16)
                    it["wb"].append(wit)
                    wb_flat.append(wit)
            it["tf"] = tf.inc(1)
            sch.append(it)

        with nc.Block() as blk:
            @blk.sync
            def _(s):
                if st0:
                    s.wait_ge(self.st.sem, st0)
                for it in sch:
                    tg, sl = it["tg"], it["slot"]
                    t0 = tg * TGC
                    if tg >= 2:
                        s.wait_ge(tf.sem, sch[tg - 2]["tf"])
                    for k in range(KC):
                        s.dma_start(out=zfh[sl][:, k * 512:(k + 1) * 512],
                                    in_=self.s_z3h[t0:t0 + TGC,
                                                   k * 128:(k + 1) * 128],
                                    transpose=True).then_inc(tr.sem, 16)
                        s.dma_start(out=zfl[sl][:, k * 512:(k + 1) * 512],
                                    in_=self.s_z3l[t0:t0 + TGC,
                                                   k * 128:(k + 1) * 128],
                                    transpose=True).then_inc(tr.sem, 16)
                    for wit in it["wb"]:
                        tc, dh, u = wit["tc"], wit["dh"], wit["u"]
                        gi = wb_flat.index(wit)
                        rows = slice(t0 + tc * 128, t0 + (tc + 1) * 128)
                        cols = slice(dh * 512, (dh + 1) * 512)
                        if gi >= 4 and (q > 0 or q == 3):
                            prev = wb_flat[gi - 4]
                            s.wait_ge(vs.sem, prev.get("vsm") or prev["vs"])
                        if q > 0:
                            s.dma_start(out=pin[u][:, :],
                                        in_=self.s_yp[rows, cols]
                                        ).then_inc(pf.sem, 16)
                        if q == 3:
                            s.dma_start(out=x2r[u][:, :],
                                        in_=self.s_x2[rows, cols]
                                        ).then_inc(pf.sem, 16)

            @blk.tensor
            def _(te):
                te.wait_ge(vs.sem, vs_w_end)
                for it in sch:
                    tg, sl = it["tg"], it["slot"]
                    te.wait_ge(tr.sem, it["tr"])
                    for jit in it["jj"]:
                        j, b, use = jit["j"], jit["b"], jit["use"]
                        self.bank_acquire(te, b, use)
                        last = None
                        for k in range(KC):
                            wh = fch[:, k * 1024 + j * 128:
                                     k * 1024 + j * 128 + 128]
                            wlo = fcl[:, k * 1024 + j * 128:
                                      k * 1024 + j * 128 + 128]
                            xh = zfh[sl][:, k * 512:(k + 1) * 512]
                            xl = zfl[sl][:, k * 512:(k + 1) * 512]
                            te.ldweights(wh)
                            te.matmul(self.ps[b][:, :], wh, xh,
                                      start=(k == 0), stop=False)
                            te.matmul(self.ps[b][:, :], wh, xl,
                                      start=False, stop=False)
                            te.ldweights(wlo)
                            last = te.matmul(self.ps[b][:, :], wlo, xh,
                                             start=False, stop=(k == KC - 1))
                        last.then_inc(self.ted[b].sem, 1)
                    te.wait_ge(vs.sem, it["h_done_vs"])
                    te.wait_ge(gp.sem, it["h_done_gp"])
                    for wit in it["wb"]:
                        tc, dh, b, use = wit["tc"], wit["dh"], wit["b"], wit["use"]
                        self.bank_acquire(te, b, use)
                        last = None
                        for j in range(8):
                            hh = hp_h[:, j * 512 + tc * 128:
                                      j * 512 + tc * 128 + 128]
                            hl = hp_l[:, j * 512 + tc * 128:
                                      j * 512 + tc * 128 + 128]
                            rh = pjh[:, j * 1024 + dh * 512:
                                     j * 1024 + dh * 512 + 512]
                            rl = pjl[:, j * 1024 + dh * 512:
                                     j * 1024 + dh * 512 + 512]
                            te.ldweights(hh)
                            te.matmul(self.ps[b][:, :], hh, rh,
                                      start=(j == 0), stop=False)
                            te.matmul(self.ps[b][:, :], hh, rl,
                                      start=False, stop=False)
                            te.ldweights(hl)
                            last = te.matmul(self.ps[b][:, :], hl, rh,
                                             start=False, stop=(j == 7))
                        last.then_inc(self.ted[b].sem, 1)
                    te.nop().then_inc(tf.sem, 1)

            @blk.scalar
            def _(sc):
                for it in sch:
                    for jit in it["jj"]:
                        j, b, use = jit["j"], jit["b"], jit["use"]
                        u = j % 2
                        gj = jj_flat.index(jit)
                        if gj >= 2:
                            sc.wait_ge(vs.sem, jj_flat[gj - 2]["vs"])
                        sc.wait_ge(self.ted[b].sem, use)
                        sc.activation(r32[u][:, :], self.ps[b][:, :], AF.Relu
                                      )
                        sc.drain(fusable=False).then_inc(ac.sem, 1)

            @blk.vector
            def _(v):
                for it in sch:
                    tg, sl = it["tg"], it["slot"]
                    if tg >= 1:
                        v.wait_ge(tf.sem, sch[tg - 1]["tf"])
                    for jit in it["jj"]:
                        j, b, use = jit["j"], jit["b"], jit["use"]
                        u = j % 2
                        v.wait_ge(ac.sem, jit["ac"])
                        gj = jj_flat.index(jit)
                        if gj >= 2:
                            v.wait_ge(gp.sem, jj_flat[gj - 2]["gp"])
                        hh = hp_h[:, j * 512:(j + 1) * 512]
                        v.tensor_tensor(hh, r32[u][:, :], self.ps[b][:, :],
                                        op=ALU.mult)
                        v.tensor_tensor(h32[u][:, :], r32[u][:, :],
                                        self.ps[b][:, :], op=ALU.mult
                                        ).then_inc(self.psf[b].sem, 1)
                        v.nop()
                        v.drain(fusable=False).then_inc(vs.sem, 1)
                    for wit in it["wb"]:
                        tc, dh, b, use = wit["tc"], wit["dh"], wit["b"], wit["use"]
                        u = wit["u"]
                        gi = wb_flat.index(wit)
                        v.wait_ge(self.ted[b].sem, use)
                        if q > 0:
                            v.wait_ge(pf.sem, wit["pf"])
                        if gi >= 4:
                            v.wait_ge(st.sem, wb_flat[gi - 4]["st"])
                        if q == 0:
                            v.tensor_copy(pout[u][:, :], self.ps[b][:, :]
                                          ).then_inc(self.psf[b].sem, 1)
                            v.nop()
                            v.drain(fusable=False).then_inc(vs.sem, 1)
                        elif q < 3:
                            v.tensor_tensor(pout[u][:, :], self.ps[b][:, :],
                                            pin[u][:, :], op=ALU.add
                                            ).then_inc(self.psf[b].sem, 1)
                            v.nop()
                            v.drain(fusable=False).then_inc(vs.sem, 1)
                        else:
                            v.tensor_tensor(y2[u][:, :], self.ps[b][:, :],
                                            pin[u][:, :], op=ALU.add
                                            ).then_inc(self.psf[b].sem, 1)
                            v.nop()
                            v.drain(fusable=False).then_inc(vs.sem, 1)
                            for bi, (bh, blo) in enumerate(_BPS):
                                i = v.tensor_scalar(inds[bi][:, :], y2[u][:, :],
                                                    bh, blo, op0=ALU.subtract,
                                                    op1=ALU.is_gt)
                            v.drain(fusable=False).then_inc(vs.sem, 1)
                            v.wait_ge(gp.sem, wit["gp"])
                            v.wait_ge(pf.sem, wit["pfx"])
                            v.tensor_add(t2b[:, :], t1b[:, :], inds[4][:, :])
                            v.scalar_tensor_tensor(ob[u][:, :], t2b[:, :], 0.2,
                                                   x2r[u][:, :], op0=ALU.mult,
                                                   op1=ALU.add
                                                   )
                            v.drain(fusable=False).then_inc(vs.sem, 1)

            @blk.gpsimd
            def _(g):
                for it in sch:
                    tg = it["tg"]
                    t0 = tg * TGC
                    if tg >= 1:
                        g.wait_ge(tf.sem, sch[tg - 1]["tf"])
                    for jit in it["jj"]:
                        j = jit["j"]
                        u = j % 2
                        g.wait_ge(vs.sem, jit["vs"])
                        hl = hp_l[:, j * 512:(j + 1) * 512]
                        hh = hp_h[:, j * 512:(j + 1) * 512]
                        g.tensor_sub(hl, h32[u][:, :], hh)
                        g.drain(fusable=False).then_inc(gp.sem, 1)
                    for wit in it["wb"]:
                        tc, dh, u = wit["tc"], wit["dh"], wit["u"]
                        rows = slice(t0 + tc * 128, t0 + (tc + 1) * 128)
                        cols = slice(dh * 512, (dh + 1) * 512)
                        if q < 3:
                            g.wait_ge(vs.sem, wit["vs"])
                            g.dma_start(out=self.s_yp[rows, cols],
                                        in_=pout[u][:, :]).then_inc(st.sem, 16)
                        else:
                            g.wait_ge(vs.sem, wit["vsi"])
                            g.tensor_add(gA[:, :], inds[0][:, :], inds[1][:, :])
                            g.tensor_add(gB[:, :], inds[2][:, :], inds[3][:, :])
                            g.drain(fusable=False)
                            g.tensor_add(t1b[:, :], gA[:, :], gB[:, :])
                            g.drain(fusable=False).then_inc(gp.sem, 1)
                            g.wait_ge(vs.sem, wit["vsm"])
                            g.dma_start(out=self.o_out[rows, cols],
                                        in_=ob[u][:, :]).then_inc(st.sem, 16)

        a32.reset(m32)
        a16.reset(m16)

    def build(self, upto=99):
        p16_0 = self.a16.mark()
        self.hsp_h = self.a16.alloc(2 * T)
        self.hsp_l = self.a16.alloc(2 * T)

        nsc = self.nc.named_scope
        with nsc("Ainit"):
            self.phase_a_init()
            self.barrier()
        if upto >= 1:
            with nsc("A1p"):
                self.phase_a1(src=0)
                self.barrier()
        if upto >= 2:
            with nsc("A2p"):
                self.phase_a2(src=0)
                self.barrier()
        if upto >= 3:
            with nsc("A1m"):
                self.phase_a1(src=1)
                self.barrier()
        if upto >= 4:
            with nsc("A2m"):
                self.phase_a2(src=1)
                self.barrier()
        if upto >= 5:
            with nsc("A3"):
                self.phase_a3()
                self.barrier()
        self.a16.reset(p16_0)
        for q in range(4):
            if upto >= 6 + q:
                with nsc(f"B{q}"):
                    self.phase_b(q)
                    self.barrier()
        for q in range(4):
            if upto >= 10 + q:
                with nsc(f"C{q}"):
                    self.phase_c(q)
                    self.barrier()
        if getattr(self, "dbg_stats", False):
            o_dbg = self.nc.dram_tensor("o_dbg", [128, 160], F32,
                                        kind="ExternalOutput").ap()
            with self.nc.Block() as blk:
                @blk.gpsimd
                def _(g):
                    g.dma_start(out=o_dbg[:, 0:64], in_=self.tiny[:, :]
                                ).then_inc(self.st.sem, 16)
                    g.dma_start(out=o_dbg[:, 64:96], in_=self.stM[:, :]
                                ).then_inc(self.st.sem, 16)
                    g.dma_start(out=o_dbg[:, 96:128], in_=self.stR[:, :]
                                ).then_inc(self.st.sem, 16)
                    g.dma_start(out=o_dbg[:, 128:160], in_=self.stR2[:, :]
                                ).then_inc(self.st.sem, 16)
                self.st.inc(64)
        with self.nc.Block() as blk:
            @blk.gpsimd
            def _(g):
                g.wait_ge(self.st.sem, self.st.n)
        self.stack.close()
        return self.nc


def _build():
    return Builder().build()


def kernel(L0_attn_x, L0_attn_log_lambda, L0_attn_B_w, L0_attn_C_w,
           L0_attn_D_skip, L0_attn_ssm_norm_w, L0_attn_ffn_norm_w,
           L0_attn_w1, L0_attn_w2, L0_attn_w3, L0_ffn_W_fc, L0_ffn_W_proj):
    x = np.ascontiguousarray(np.asarray(L0_attn_x, dtype=np.float32))
    B, S, _ = x.shape
    assert (B, S) == (4, 8192)
    tw = lambda a: np.ascontiguousarray(np.asarray(a, dtype=np.float32).T)
    com = {
        "BwT": tw(L0_attn_B_w), "CwT": tw(L0_attn_C_w),
        "w1T": tw(L0_attn_w1), "w3T": tw(L0_attn_w3), "w2T": tw(L0_attn_w2),
        "fcT": tw(L0_ffn_W_fc), "projT": tw(L0_ffn_W_proj),
        "log_lambda": np.ascontiguousarray(np.asarray(L0_attn_log_lambda, np.float32)),
        "D_skip": np.ascontiguousarray(np.asarray(L0_attn_D_skip, np.float32)),
        "ssm_w": np.ascontiguousarray(np.asarray(L0_attn_ssm_norm_w, np.float32)),
        "ffn_w": np.ascontiguousarray(np.asarray(L0_attn_ffn_norm_w, np.float32)),
    }
    zeros = np.zeros((T, D), dtype=np.float32)
    in_maps = []
    for c in range(8):
        seq, half = c // 2, c % 2
        m = dict(com)
        m["x"] = np.ascontiguousarray(x[seq, half * T:(half + 1) * T])
        m["xprev"] = np.ascontiguousarray(x[seq, 0:T]) if half else zeros
        in_maps.append(m)
    nc = _build()
    res = run_bass_kernel_spmd(nc, in_maps, core_ids=list(range(8)))
    out = np.empty((4, 8192, 1024), dtype=np.float32)
    for c in range(8):
        seq, half = c // 2, c % 2
        out[seq, half * T:(half + 1) * T] = res.results[c]["out"]
    return out
